# version 01
# baseline (speedup 1.0000x reference)
import numpy as np
import concourse.bass as bass
import concourse.mybir as mybir
from concourse.bass_utils import run_bass_kernel_spmd
from contextlib import ExitStack

F32 = mybir.dt.float32
BF16 = mybir.dt.bfloat16
I32 = mybir.dt.int32
AF = mybir.ActivationFunctionType
ALU = mybir.AluOpType
AX = mybir.AxisListType.X

NCORES = 8
T = 16384
D = 1024
CH = 8
FF = 4096
FH = 3072
FHT = FH // 128
FFT = FF // 128
TOK = T // NCORES
J = TOK + 1
NEG = -1.0e38
EPS = 1e-5
C127_3 = float(1.0 / (127.0 ** 3))


class K:
    ENGINES = ("sync", "gpsimd", "vector", "scalar", "tensor")

    PROG = {"vector": "sv", "scalar": "ss", "gpsimd": "sg", "tensor": "st"}

    def __init__(self, nc):
        self.nc = nc
        self.streams = {e: [] for e in self.ENGINES}
        self.led = {}
        self.sems = {}
        self.autofence = False

    def sem(self, ctx, name):
        self.sems[name] = ctx.enter_context(self.nc.semaphore(name))
        self.led[name] = 0

    def op(self, eng, fn, inc=None, n=1):
        fence = self.autofence and eng in ("vector", "scalar")
        if fence and inc is None:
            inc = self.PROG[eng]
        if inc is not None:
            self.led[inc] += n
        self.streams[eng].append(("i", fn, inc, n))
        if fence:
            self.wait(eng, inc, self.led[inc])
        return self.led[inc] if inc else None

    from contextlib import contextmanager as _cm

    @_cm
    def af(self):
        prev = self.autofence
        self.autofence = True
        try:
            yield
        finally:
            self.autofence = prev

    def dma(self, eng, out, in_, inc="qs", **kw):
        fn = (lambda out=out, in_=in_, kw=kw:
              (lambda h: h.dma_start(out=out, in_=in_, **kw)))()
        return self.op(eng, fn, inc=inc, n=16)

    def wait(self, eng, sem, val):
        if val > 0:
            self.streams[eng].append(("w", sem, val))

    def now(self, sem):
        return self.led[sem]

    def tile_barrier(self, sems=("qs", "sv", "ss", "st")):
        counts = [(s, self.led[s]) for s in sems]
        for eng in self.ENGINES:
            for s, v in counts:
                self.wait(eng, s, v)

    def emit(self, block):
        def mk(eng_name):
            stream = self.streams[eng_name]
            sems = self.sems

            def body(h):
                for item in stream:
                    if item[0] == "i":
                        _, fn, inc, n = item
                        ins = fn(h)
                        if inc is not None:
                            ins.then_inc(sems[inc], n)
                    else:
                        _, sem, val = item
                        h.wait_ge(sems[sem], val)
            return body
        block.sync(mk("sync"))
        block.gpsimd(mk("gpsimd"))
        block.vector(mk("vector"))
        block.scalar(mk("scalar"))
        block.tensor(mk("tensor"))


class Phase:

    def __init__(self, nc):
        self.nc = nc
        self.stack = ExitStack()

    def sb(self, name, shape, dt):
        return self.stack.enter_context(self.nc.sbuf_tensor(name, shape, dt))

    def close(self):
        self.stack.close()


def ln_tokmajor(k, blk_in, blk_out, npart, scr, s1, s2, st1, st2, st3,
                pre=None):
    with k.af():
        return _ln_tokmajor(k, blk_in, blk_out, npart, scr, s1, s2, st1, st2, st3)


def _ln_tokmajor(k, blk_in, blk_out, npart, scr, s1, s2, st1, st2, st3):
    k.op("scalar", lambda h: h.activation(
        scr[:npart, :], blk_in, AF.Square, accum_out=s2[:npart, :]), inc="ss")
    k.op("scalar", lambda h: h.activation(
        scr[:npart, :], blk_in, AF.Identity, accum_out=s1[:npart, :]), inc="ss")
    k.wait("vector", "ss", k.now("ss"))
    k.op("vector", lambda h: h.tensor_scalar(
        st1[:npart, :], s1[:npart, :], 1.0 / D, None, ALU.mult), inc="sv")
    k.op("vector", lambda h: h.tensor_mul(
        st2[:npart, :], st1[:npart, :], st1[:npart, :]), inc="sv")
    k.op("vector", lambda h: h.scalar_tensor_tensor(
        st2[:npart, :], s2[:npart, :], 1.0 / D, st2[:npart, :],
        ALU.mult, ALU.subtract), inc="sv")
    k.op("vector", lambda h: h.tensor_scalar(
        st2[:npart, :], st2[:npart, :], 1.0, EPS, ALU.mult, ALU.add), inc="sv")
    k.wait("scalar", "sv", k.now("sv"))
    k.op("scalar", lambda h: h.activation(
        st2[:npart, :], st2[:npart, :], AF.Ln), inc="ss")
    k.op("scalar", lambda h: h.activation(
        st2[:npart, :], st2[:npart, :], AF.Exp, scale=-0.5), inc="ss")
    k.wait("vector", "ss", k.now("ss"))
    k.op("vector", lambda h: h.tensor_scalar(
        st3[:npart, :], st1[:npart, :], st2[:npart, :], -1.0,
        ALU.mult, ALU.mult), inc="sv")
    k.wait("scalar", "sv", k.now("sv"))
    k.op("scalar", lambda h: h.activation(
        blk_out, blk_in, AF.Identity, bias=st3[:npart, :],
        scale=st2[:npart, :]), inc="ss")
    return k.now("ss")


def build_launch1():
    nc = bass.Bass("TRN2", target_bir_lowering=False)
    x_a1 = nc.dram_tensor("x_a1", [TOK + 2, D], F32, kind="ExternalInput").ap()
    wk = nc.dram_tensor("wk", [D, D], F32, kind="ExternalInput").ap()
    wv = nc.dram_tensor("wv", [D, D], F32, kind="ExternalInput").ap()
    wr = nc.dram_tensor("wr", [D, D], F32, kind="ExternalInput").ap()
    wg_i = nc.dram_tensor("wg_i", [FH, D], I32, kind="ExternalInput").ap()
    wu_i = nc.dram_tensor("wu_i", [FH, D], I32, kind="ExternalInput").ap()
    wd_i = nc.dram_tensor("wd_i", [D, FH], I32, kind="ExternalInput").ap()
    vecs = {}
    for nm in ("ln1_g", "ln1_b", "mix_k", "mix_v", "mix_r", "tdec"):
        vecs[nm] = nc.dram_tensor(nm, [128, CH], F32, kind="ExternalInput").ap()
    hmask = nc.dram_tensor("hmask", [128, 1], F32, kind="ExternalInput").ap()
    kbias = nc.dram_tensor("kbias", [128, 1], F32, kind="ExternalInput").ap()
    ident = nc.dram_tensor("ident", [128, 128], F32, kind="ExternalInput").ap()
    ramp = nc.dram_tensor("ramp", [128, 128], F32, kind="ExternalInput").ap()

    xn_sp = nc.dram_tensor("xn_sp", [128, CH, J], F32, kind="ExternalOutput").ap()
    k_sp = nc.dram_tensor("k_sp", [128, CH, J], F32, kind="ExternalOutput").ap()
    v_sp = nc.dram_tensor("v_sp", [128, CH, J], F32, kind="ExternalOutput").ap()
    r_sp = nc.dram_tensor("r_sp", [128, CH, J], F32, kind="ExternalOutput").ap()
    g_sp = nc.dram_tensor("g_sp", [128, CH, TOK], F32, kind="ExternalOutput").ap()
    summ = nc.dram_tensor("summ", [3, 128, CH], F32, kind="ExternalOutput").ap()
    wg_b = nc.dram_tensor("wg_b", [FH, D], BF16, kind="Internal").ap()
    wu_b = nc.dram_tensor("wu_b", [FH, D], BF16, kind="Internal").ap()
    wd_b = nc.dram_tensor("wd_b", [D, FH], BF16, kind="Internal").ap()

    with ExitStack() as ctx:
        k = K(nc)
        for s in ("qs", "qg", "sv", "ss", "st"):
            k.sem(ctx, s)
        sb = lambda nm, shape, dt: ctx.enter_context(nc.sbuf_tensor(nm, shape, dt))
        psb = lambda nm, w: ctx.enter_context(nc.psum_tensor(nm, [128, w], F32))

        tabs = {}
        for nm, ap_ in vecs.items():
            tabs[nm] = sb(nm + "_t", [128, CH], F32)
            k.dma("sync", tabs[nm][:, :], ap_)
        negw = sb("negw", [128, CH], F32)
        hm_t = sb("hm_t", [128, 1], F32)
        kb_t = sb("kb_t", [128, 1], F32)
        id_t = sb("id_t", [128, 128], F32)
        ramp_t = sb("ramp_t", [128, 128], F32)
        k.dma("sync", hm_t[:, :], hmask)
        k.dma("sync", kb_t[:, :], kbias)
        k.dma("sync", id_t[:, :], ident)
        k.dma("sync", ramp_t[:, :], ramp)
        s1 = sb("s1", [128, 1], F32)
        s2 = sb("s2", [128, 1], F32)
        st1 = sb("st1", [128, 1], F32)
        st2 = sb("st2", [128, 1], F32)
        st3 = sb("st3", [128, 1], F32)
        h1_carry = sb("h1_carry", [128, CH], F32)
        ps_a = psb("ps_a", D)
        ps_b = psb("ps_b", D)
        ps_m = [psb(f"ps_m{i}", 512) for i in range(2)]

        k.wait("scalar", "qs", k.now("qs"))
        k.op("scalar", lambda h: h.activation(
            negw[:, :], tabs["tdec"][:, :], AF.Exp), inc="ss")

        k.dma("gpsimd", wg_b, wg_i, "qg")
        k.dma("gpsimd", wu_b, wu_i, "qg")
        k.dma("gpsimd", wd_b, wd_i, "qg")
        ffn_cast_done = k.now("qg")

        A = Phase(nc)
        wk_s = A.sb("wk_s", [128, CH, D], BF16)
        wv_s = A.sb("wv_s", [128, CH, D], BF16)
        wr_s = A.sb("wr_s", [128, CH, D], BF16)
        for w_hbm, w_sb in ((wk, wk_s), (wv, wv_s), (wr, wr_s)):
            k.dma("gpsimd", w_sb[:, :, :],
                  w_hbm.rearrange("(kk p) e -> p kk e", p=128), "qg")
        a1_w_loaded = k.now("qg")
        k.wait("tensor", "qg", a1_w_loaded)

        x_tm = A.sb("x_tm", [128, 4, D], F32)
        t_tm = A.sb("t_tm", [128, D], F32)
        scr_tm = A.sb("scr_tm", [128, D], F32)
        xn_cm = A.sb("xn_cm", [128, CH, 512], F32)
        h1_cm = A.sb("h1_cm", [128, CH, 512], F32)
        dh = A.sb("dh", [128, CH, 512], F32)
        u_b = [A.sb(f"u_b{i}", [128, CH, 512], BF16) for i in range(3)]
        out_cm = {"k": A.sb("kcm", [128, CH, 512], F32),
                  "v": A.sb("vcm", [128, CH, 512], F32),
                  "r": A.sb("rcm", [128, CH, 512], F32)}
        spill_t = {"k": k_sp, "v": v_sp, "r": r_sp}

        tiles = [(512 * t0, 512) for t0 in range(4)] + [(TOK, 2)]
        evac_ev = {0: 0, 1: 0}
        prev_evac = {0: 0, 1: 0}
        for ti, (tl0, n) in enumerate(tiles):
            nb = (n + 127) // 128
            for b in range(nb):
                npart = min(128, n - 128 * b)
                k.dma("sync", x_tm[:npart, b, :],
                      x_a1[tl0 + 128 * b: tl0 + 128 * b + npart, :])
            k.wait("scalar", "qs", k.now("qs"))
            for b in range(nb):
                npart = min(128, n - 128 * b)
                blk = x_tm[:npart, b, :]
                ln_tokmajor(k, blk, blk, npart, scr_tm, s1, s2, st1, st2, st3)
                xn_done = ln_tokmajor(k, blk, t_tm[:npart, :], npart,
                                      scr_tm, s1, s2, st1, st2, st3)
                k.wait("tensor", "ss", xn_done)
                for c in range(CH):
                    k.op("tensor", lambda h, c=c, npart=npart, blk=blk: h.transpose(
                        ps_a[:, 128 * c:128 * c + npart],
                        blk[:, 128 * c:128 * (c + 1)], id_t[:npart, :npart]),
                        inc="st")
                for c in range(CH):
                    k.op("tensor", lambda h, c=c, npart=npart: h.transpose(
                        ps_b[:, 128 * c:128 * c + npart],
                        t_tm[:npart, 128 * c:128 * (c + 1)], id_t[:npart, :npart]),
                        inc="st")
                k.wait("vector", "st", k.now("st"))
                ps_a_v = ps_a.ap().rearrange("p (c q) -> p c q", c=CH)[:, :, :npart]
                k.op("vector", lambda h, b=b, npart=npart, s=ps_a_v: h.tensor_copy(
                    xn_cm.ap()[:, :, 128 * b:128 * b + npart], s), inc="sv")
                for c in range(CH):
                    k.op("vector", lambda h, c=c, b=b, npart=npart: h.tensor_scalar(
                        h1_cm[:, c, 128 * b:128 * b + npart],
                        ps_b[:, 128 * c:128 * c + npart],
                        tabs["ln1_g"][:, c:c + 1], tabs["ln1_b"][:, c:c + 1],
                        ALU.mult, ALU.add), inc="sv")
                k.wait("tensor", "sv", k.now("sv"))
            if ti == 0:
                with k.af():
                    k.op("vector", lambda h: h.tensor_scalar(
                        h1_cm[:, :, 1:2], h1_cm[:, :, 1:2], hm_t[:, 0:1], None,
                        ALU.mult), inc="sv")
            k.wait("sync", "sv", k.now("sv"))
            if tl0 == 0:
                k.dma("sync", xn_sp[:, :, 0:n - 1], xn_cm.ap()[:, :, 1:n])
            else:
                k.dma("sync", xn_sp[:, :, tl0 - 1:tl0 - 1 + n],
                      xn_cm.ap()[:, :, 0:n])

            u0 = max(1, tl0)
            cnt = tl0 + n - u0
            off = u0 - tl0
            j0 = u0 - 1
            if off == 0:
                with k.af():
                    k.op("vector", lambda h: h.tensor_sub(
                        dh[:, :, 0:1], h1_cm[:, :, 0:1],
                        h1_carry.ap().unsqueeze(2)), inc="sv")
                if cnt > 1:
                    k.op("vector", lambda h, cnt=cnt: h.tensor_sub(
                        dh[:, :, 1:cnt], h1_cm[:, :, 1:cnt],
                        h1_cm[:, :, 0:cnt - 1]), inc="sv")
            else:
                k.op("vector", lambda h, off=off, cnt=cnt: h.tensor_sub(
                    dh[:, :, 0:cnt], h1_cm[:, :, off:off + cnt],
                    h1_cm[:, :, off - 1:off - 1 + cnt]), inc="sv")

            for mi, (mtn, w_sb, spname) in enumerate((
                    ("mix_k", wk_s, "k"), ("mix_v", wv_s, "v"),
                    ("mix_r", wr_s, "r"))):
                mt = tabs[mtn]
                ub = u_b[mi]
                ocm = out_cm[spname]
                for c in range(CH):
                    if off == 0:
                        with k.af():
                            k.op("vector", lambda h, c=c, mt=mt, ub=ub:
                                 h.scalar_tensor_tensor(
                                     ub[:, c, 0:1], dh[:, c, 0:1], mt[:, c:c + 1],
                                     h1_carry.ap()[:, c:c + 1].unsqueeze(2),
                                     ALU.mult, ALU.add), inc="sv")
                        if cnt > 1:
                            k.op("vector", lambda h, c=c, mt=mt, ub=ub, cnt=cnt:
                                 h.scalar_tensor_tensor(
                                     ub[:, c, 1:cnt], dh[:, c, 1:cnt],
                                     mt[:, c:c + 1], h1_cm[:, c, 0:cnt - 1],
                                     ALU.mult, ALU.add), inc="sv")
                    else:
                        k.op("vector", lambda h, c=c, mt=mt, ub=ub, off=off,
                             cnt=cnt: h.scalar_tensor_tensor(
                                 ub[:, c, 0:cnt], dh[:, c, 0:cnt],
                                 mt[:, c:c + 1],
                                 h1_cm[:, c, off - 1:off - 1 + cnt],
                                 ALU.mult, ALU.add), inc="sv")
                mix_done = k.now("sv")
                k.wait("tensor", "sv", mix_done)
                for o in range(CH):
                    bank = ps_m[o % 2]
                    k.wait("tensor", "sv", evac_ev[o % 2])
                    for kt in range(CH):
                        k.op("tensor", lambda h, o=o, kt=kt, w_sb=w_sb, ub=ub,
                             cnt=cnt, bank=bank: h.matmul(
                                 bank[:, 0:cnt],
                                 w_sb[:, kt, 128 * o:128 * (o + 1)],
                                 ub[:, kt, 0:cnt],
                                 start=(kt == 0), stop=(kt == 7)), inc="st")
                    k.wait("vector", "st", k.now("st"))
                    k.op("vector", lambda h, o=o, ocm=ocm, cnt=cnt, bank=bank:
                         h.tensor_copy(ocm[:, o, 0:cnt], bank[:, 0:cnt]),
                         inc="sv")
                    evac_ev[o % 2] = k.now("sv")
                prev_evac = dict(evac_ev)
                last_evac = k.now("sv")
                if spname == "k" and ti == 0:
                    with k.af():
                        k.op("vector", lambda h: h.tensor_scalar(
                            out_cm["k"][:, :, 0:1], out_cm["k"][:, :, 0:1],
                            hm_t[:, 0:1], kb_t[:, 0:1], ALU.mult, ALU.add),
                            inc="sv")
                    last_evac = k.now("sv")
                k.wait("sync", "sv", last_evac)
                k.dma("sync", spill_t[spname][:, :, j0:j0 + cnt],
                      ocm.ap()[:, :, 0:cnt])
            with k.af():
                k.op("vector", lambda h, n=n: h.tensor_copy(
                    h1_carry[:, :], h1_cm.ap()[:, :, n - 1]), inc="sv")
            k.tile_barrier()
        A.close()

        S = Phase(nc)
        sA = S.sb("sA", [128, CH], F32)
        sB = S.sb("sB", [128, CH], F32)
        sP = S.sb("sP", [128, CH], F32)
        tA = S.sb("tA", [128, CH], F32)
        tB = S.sb("tB", [128, CH], F32)
        tP = S.sb("tP", [128, CH], F32)
        f1 = S.sb("f1", [128, CH], F32)
        f2 = S.sb("f2", [128, CH], F32)
        m_buf = S.sb("m_buf", [128, CH, 128], F32)
        e_buf = S.sb("e_buf", [128, CH, 128], F32)
        kg_b = S.sb("kg_b", [128, CH, 128], F32)
        vg_b = S.sb("vg_b", [128, CH, 128], F32)
        negM = S.sb("negM", [128, CH], F32)
        M_b = S.sb("M_b", [128, CH], F32)
        k.autofence = True
        k.op("vector", lambda h: h.memset(sA[:, :], 0.0), inc="sv")
        k.op("vector", lambda h: h.memset(sB[:, :], 0.0), inc="sv")
        k.op("vector", lambda h: h.memset(sP[:, :], NEG), inc="sv")
        for g in range(16):
            k.dma("sync", kg_b[:, :, :], k_sp[:, :, 128 * g:128 * (g + 1)])
            k.dma("sync", vg_b[:, :, :], v_sp[:, :, 128 * g:128 * (g + 1)])
            k.wait("vector", "qs", k.now("qs"))
            for c in range(CH):
                k.op("vector", lambda h, c=c: h.scalar_tensor_tensor(
                    m_buf[:, c, :], ramp_t[:, :], negw[:, c:c + 1],
                    kg_b[:, c, :], ALU.mult, ALU.add), inc="sv")
                k.op("vector", lambda h, c=c: h.reduce_max(
                    M_b[:, c:c + 1], m_buf[:, c, :], axis=AX), inc="sv")
            k.op("vector", lambda h: h.tensor_scalar(
                negM[:, :], M_b[:, :], -1.0, None, ALU.mult), inc="sv")
            k.wait("scalar", "sv", k.now("sv"))
            for c in range(CH):
                k.op("scalar", lambda h, c=c: h.activation(
                    e_buf[:, c, :], m_buf[:, c, :], AF.Exp,
                    bias=negM[:, c:c + 1], accum_out=tB[:, c:c + 1]), inc="ss")
            k.wait("vector", "ss", k.now("ss"))
            for c in range(CH):
                k.op("vector", lambda h, c=c: h.tensor_mul(
                    e_buf[:, c, :], e_buf[:, c, :], vg_b[:, c, :]), inc="sv")
                k.op("vector", lambda h, c=c: h.reduce_sum(
                    tA[:, c:c + 1], e_buf[:, c, :], axis=AX), inc="sv")
            k.op("vector", lambda h: h.scalar_tensor_tensor(
                tP[:, :], negw[:, :], 127.0, M_b[:, :], ALU.mult,
                ALU.subtract), inc="sv")
            k.op("vector", lambda h: h.tensor_scalar(
                tP[:, :], tP[:, :], -1.0, None, ALU.mult), inc="sv")
            k.op("vector", lambda h: h.scalar_tensor_tensor(
                f1[:, :], negw[:, :], -128.0, sP[:, :], ALU.mult, ALU.add),
                inc="sv")
            k.op("vector", lambda h: h.tensor_max(
                f2[:, :], f1[:, :], tP[:, :]), inc="sv")
            k.op("vector", lambda h: h.tensor_sub(f1[:, :], f1[:, :], f2[:, :]),
                 inc="sv")
            k.op("vector", lambda h: h.tensor_sub(tP[:, :], tP[:, :], f2[:, :]),
                 inc="sv")
            k.wait("scalar", "sv", k.now("sv"))
            k.op("scalar", lambda h: h.activation(f1[:, :], f1[:, :], AF.Exp),
                 inc="ss")
            k.op("scalar", lambda h: h.activation(tP[:, :], tP[:, :], AF.Exp),
                 inc="ss")
            k.wait("vector", "ss", k.now("ss"))
            for acc, tnew in ((sA, tA), (sB, tB)):
                k.op("vector", lambda h, acc=acc: h.tensor_mul(
                    acc[:, :], acc[:, :], f1[:, :]), inc="sv")
                k.op("vector", lambda h, tnew=tnew: h.tensor_mul(
                    tnew[:, :], tnew[:, :], tP[:, :]), inc="sv")
                k.op("vector", lambda h, acc=acc, tnew=tnew: h.tensor_add(
                    acc[:, :], acc[:, :], tnew[:, :]), inc="sv")
            k.op("vector", lambda h: h.tensor_copy(sP[:, :], f2[:, :]), inc="sv")
            k.wait("sync", "sv", k.now("sv"))
        k.autofence = False
        k.dma("sync", summ[0, :, :], sA[:, :])
        k.dma("sync", summ[1, :, :], sB[:, :])
        k.dma("sync", summ[2, :, :], sP[:, :])
        k.tile_barrier()
        S.close()

        F = Phase(nc)
        wgT = F.sb("wgT", [128, CH, FH], BF16)
        wuT = F.sb("wuT", [128, CH, FH], BF16)
        wdT = F.sb("wdT", [128, FHT, D], BF16)
        k.wait("sync", "qg", ffn_cast_done)
        for c in range(CH):
            k.op("sync", lambda h, c=c: h.dma_start(
                out=wgT[:, c, :], in_=wg_b[:, 128 * c:128 * (c + 1)],
                transpose=True), inc="qs", n=16)
            k.op("sync", lambda h, c=c: h.dma_start(
                out=wuT[:, c, :], in_=wu_b[:, 128 * c:128 * (c + 1)],
                transpose=True), inc="qs", n=16)
        for hc in range(FHT):
            k.op("sync", lambda h, hc=hc: h.dma_start(
                out=wdT[:, hc, :], in_=wd_b[:, 128 * hc:128 * (hc + 1)],
                transpose=True), inc="qs", n=16)
        ffn_w_loaded = k.now("qs")
        k.wait("tensor", "qs", ffn_w_loaded)
        k.wait("gpsimd", "qs", ffn_w_loaded)

        xn_bf = F.sb("xn_bf", [128, CH, 512], BF16)
        hh_bf = F.sb("hh_bf", [128, FHT, 512], BF16)
        sg_scr = F.sb("sg_scr", [128, 512], F32)
        g_ev = F.sb("g_ev", [128, 2, 512], F32)
        ps_g = psb("ps_g", 512)
        ps_u = psb("ps_u", 512)
        gev_ev = {0: 0, 1: 0}
        for b2 in range(4):
            m0 = 512 * b2
            k.dma("gpsimd", xn_bf[:, :, :], xn_sp[:, :, m0 + 1:m0 + 513], "qg")
            k.wait("tensor", "qg", k.now("qg"))
            for ho in range(FHT):
                for kt in range(CH):
                    k.op("tensor", lambda h, ho=ho, kt=kt: h.matmul(
                        ps_g[:, :], wgT[:, kt, 128 * ho:128 * (ho + 1)],
                        xn_bf[:, kt, :], start=(kt == 0), stop=(kt == 7)),
                        inc="st")
                g_mm = k.now("st")
                for kt in range(CH):
                    k.op("tensor", lambda h, ho=ho, kt=kt: h.matmul(
                        ps_u[:, :], wuT[:, kt, 128 * ho:128 * (ho + 1)],
                        xn_bf[:, kt, :], start=(kt == 0), stop=(kt == 7)),
                        inc="st")
                u_mm = k.now("st")
                k.wait("vector", "st", g_mm)
                k.op("vector", lambda h: h.tensor_scalar(
                    sg_scr[:, :], ps_g[:, :], -1.0 / 127.0, 25.0,
                    ALU.mult, ALU.min), inc="sv")
                k.wait("scalar", "sv", k.now("sv"))
                k.op("scalar", lambda h: h.activation(
                    sg_scr[:, :], sg_scr[:, :], AF.Exp), inc="ss")
                k.op("scalar", lambda h: h.activation(
                    sg_scr[:, :], sg_scr[:, :], AF.Ln, bias=1.0), inc="ss")
                k.op("scalar", lambda h: h.activation(
                    sg_scr[:, :], sg_scr[:, :], AF.Exp, scale=-1.0), inc="ss")
                k.wait("vector", "ss", k.now("ss"))
                k.wait("vector", "st", u_mm)
                k.op("vector", lambda h: h.tensor_mul(
                    sg_scr[:, :], sg_scr[:, :], ps_g[:, :]), inc="sv")
                k.op("vector", lambda h, ho=ho: h.tensor_mul(
                    hh_bf[:, ho, :], sg_scr[:, :], ps_u[:, :]), inc="sv")
                k.wait("tensor", "sv", k.now("sv"))
            hh_done = k.now("sv")
            k.wait("tensor", "sv", hh_done)
            for do in range(CH):
                bank = ps_m[do % 2]
                k.wait("tensor", "qs", gev_ev[do % 2])
                for kt in range(FHT):
                    k.op("tensor", lambda h, do=do, kt=kt, bank=bank: h.matmul(
                        bank[:, :], wdT[:, kt, 128 * do:128 * (do + 1)],
                        hh_bf[:, kt, :], start=(kt == 0), stop=(kt == FHT - 1)),
                        inc="st")
                k.wait("vector", "st", k.now("st"))
                k.op("vector", lambda h, bank=bank: h.tensor_scalar(
                    sg_scr[:, :], bank[:, :], -C127_3, 25.0,
                    ALU.mult, ALU.min), inc="sv")
                k.wait("scalar", "sv", k.now("sv"))
                k.op("scalar", lambda h: h.activation(
                    sg_scr[:, :], sg_scr[:, :], AF.Exp), inc="ss")
                k.op("scalar", lambda h: h.activation(
                    sg_scr[:, :], sg_scr[:, :], AF.Ln, bias=1.0), inc="ss")
                k.op("scalar", lambda h: h.activation(
                    sg_scr[:, :], sg_scr[:, :], AF.Exp, scale=-1.0), inc="ss")
                k.wait("vector", "ss", k.now("ss"))
                slot = do % 2
                k.op("vector", lambda h, slot=slot, bank=bank: h.tensor_mul(
                    g_ev[:, slot, :], sg_scr[:, :], bank[:, :]), inc="sv")
                k.wait("sync", "sv", k.now("sv"))
                k.dma("sync", g_sp[:, do, m0:m0 + 512], g_ev.ap()[:, slot, :])
                gev_ev[slot] = k.now("qs")
            k.tile_barrier()
        F.close()

        k.wait("sync", "qs", k.now("qs"))
        k.wait("sync", "qg", k.now("qg"))
        block = ctx.enter_context(nc.Block())
        k.emit(block)
    return nc


def build_launch2():
    nc = bass.Bass("TRN2", target_bir_lowering=False)
    x_b = nc.dram_tensor("x_b", [TOK, D], F32, kind="ExternalInput").ap()
    xn_sp = nc.dram_tensor("xn_sp", [128, CH, J], F32, kind="ExternalInput").ap()
    k_sp = nc.dram_tensor("k_sp", [128, CH, J], F32, kind="ExternalInput").ap()
    v_sp = nc.dram_tensor("v_sp", [128, CH, J], F32, kind="ExternalInput").ap()
    r_sp = nc.dram_tensor("r_sp", [128, CH, J], F32, kind="ExternalInput").ap()
    g_sp = nc.dram_tensor("g_sp", [128, CH, TOK], F32, kind="ExternalInput").ap()
    summ_all = nc.dram_tensor("summ_all", [128, NCORES, 3, CH], F32,
                              kind="ExternalInput").ap()
    wo = nc.dram_tensor("wo", [D, D], F32, kind="ExternalInput").ap()
    cwr = nc.dram_tensor("cwr", [D, D], F32, kind="ExternalInput").ap()
    cwk = nc.dram_tensor("cwk", [D, FF], F32, kind="ExternalInput").ap()
    cwv = nc.dram_tensor("cwv", [FF, D], F32, kind="ExternalInput").ap()
    vecs = {}
    for nm in ("ln2_g", "ln2_b", "cmix_k", "cmix_r", "tdec", "tfirst"):
        vecs[nm] = nc.dram_tensor(nm, [128, CH], F32, kind="ExternalInput").ap()
    hmask = nc.dram_tensor("hmask", [128, 1], F32, kind="ExternalInput").ap()
    cmask = nc.dram_tensor("cmask", [128, NCORES], F32, kind="ExternalInput").ap()
    ident = nc.dram_tensor("ident", [128, 128], F32, kind="ExternalInput").ap()
    out = nc.dram_tensor("out", [TOK, D], F32, kind="ExternalOutput").ap()
    xmid_sp = nc.dram_tensor("xmid_sp", [128, CH, TOK], F32, kind="Internal").ap()
    uck_sp = nc.dram_tensor("uck_sp", [128, CH, TOK], BF16, kind="Internal").ap()
    rr_sp = nc.dram_tensor("rr_sp", [128, CH, TOK], F32, kind="Internal").ap()
    a2rr_sp = nc.dram_tensor("a2rr_sp", [128, CH, TOK], F32, kind="Internal").ap()

    with ExitStack() as ctx:
        k = K(nc)
        for s in ("qs", "qg", "sv", "ss", "st"):
            k.sem(ctx, s)
        sb = lambda nm, shape, dt: ctx.enter_context(nc.sbuf_tensor(nm, shape, dt))
        psb = lambda nm, w: ctx.enter_context(nc.psum_tensor(nm, [128, w], F32))

        tabs = {}
        for nm, ap_ in vecs.items():
            tabs[nm] = sb(nm + "_t", [128, CH], F32)
            k.dma("sync", tabs[nm][:, :], ap_)
        negw = sb("negw", [128, CH], F32)
        w_t = sb("w_t", [128, CH], F32)
        U_t = sb("U_t", [128, CH], F32)
        hm_t = sb("hm_t", [128, 1], F32)
        cm_t = sb("cm_t", [128, NCORES], F32)
        id_t = sb("id_t", [128, 128], F32)
        k.dma("sync", hm_t[:, :], hmask)
        k.dma("sync", cm_t[:, :], cmask)
        k.dma("sync", id_t[:, :], ident)
        s1 = sb("s1", [128, 1], F32)
        s2 = sb("s2", [128, 1], F32)
        st1 = sb("st1", [128, 1], F32)
        st2 = sb("st2", [128, 1], F32)
        st3 = sb("st3", [128, 1], F32)
        cPP = sb("cPP", [128, CH], F32)
        cAA = sb("cAA", [128, CH], F32)
        cBB = sb("cBB", [128, CH], F32)
        cH2 = sb("cH2", [128, CH], F32)
        ps_a = psb("ps_a", D)
        ps_b = psb("ps_b", D)
        ps_m = [psb(f"ps_m{i}", 512) for i in range(2)]
        ps_k = psb("ps_k", 512)

        k.wait("scalar", "qs", k.now("qs"))
        k.op("scalar", lambda h: h.activation(
            negw[:, :], tabs["tdec"][:, :], AF.Exp), inc="ss")
        k.op("scalar", lambda h: h.activation(
            U_t[:, :], tabs["tfirst"][:, :], AF.Exp), inc="ss")
        k.wait("vector", "ss", k.now("ss"))
        k.op("vector", lambda h: h.tensor_scalar(
            w_t[:, :], negw[:, :], -1.0, None, ALU.mult), inc="sv")

        G = Phase(nc)
        summ_b = G.sb("summ_b", [128, NCORES, 3, CH], F32)
        f1 = G.sb("f1", [128, CH], F32)
        f2 = G.sb("f2", [128, CH], F32)
        f3 = G.sb("f3", [128, CH], F32)
        f4 = G.sb("f4", [128, CH], F32)
        f5 = G.sb("f5", [128, CH], F32)
        k.dma("sync", summ_b[:, :, :, :], summ_all)
        k.autofence = True
        k.op("vector", lambda h: h.memset(cAA[:, :], 0.0), inc="sv")
        k.op("vector", lambda h: h.memset(cBB[:, :], 0.0), inc="sv")
        k.op("vector", lambda h: h.memset(cPP[:, :], NEG), inc="sv")
        k.wait("vector", "qs", k.now("qs"))
        for jc in range(NCORES - 1):
            k.op("vector", lambda h: h.scalar_tensor_tensor(
                f1[:, :], negw[:, :], -2048.0, cPP[:, :], ALU.mult, ALU.add),
                inc="sv")
            k.op("vector", lambda h, jc=jc: h.tensor_max(
                f2[:, :], f1[:, :], summ_b[:, jc, 2, :]), inc="sv")
            k.op("vector", lambda h: h.tensor_sub(
                f1[:, :], f1[:, :], f2[:, :]), inc="sv")
            k.op("vector", lambda h, jc=jc: h.tensor_sub(
                f3[:, :], summ_b[:, jc, 2, :], f2[:, :]), inc="sv")
            k.wait("scalar", "sv", k.now("sv"))
            k.op("scalar", lambda h: h.activation(f1[:, :], f1[:, :], AF.Exp),
                 inc="ss")
            k.op("scalar", lambda h: h.activation(f3[:, :], f3[:, :], AF.Exp),
                 inc="ss")
            k.wait("vector", "ss", k.now("ss"))
            for acc, idx in ((cAA, 0), (cBB, 1)):
                k.op("vector", lambda h, acc=acc: h.tensor_mul(
                    f4[:, :], acc[:, :], f1[:, :]), inc="sv")
                k.op("vector", lambda h, jc=jc, idx=idx: h.tensor_mul(
                    f5[:, :], summ_b[:, jc, idx, :], f3[:, :]), inc="sv")
                k.op("vector", lambda h: h.tensor_add(
                    f4[:, :], f4[:, :], f5[:, :]), inc="sv")
                k.op("vector", lambda h, acc=acc: h.tensor_sub(
                    f4[:, :], f4[:, :], acc[:, :]), inc="sv")
                k.op("vector", lambda h, acc=acc, jc=jc: h.scalar_tensor_tensor(
                    acc[:, :], f4[:, :], cm_t[:, jc:jc + 1], acc[:, :],
                    ALU.mult, ALU.add), inc="sv")
            k.op("vector", lambda h: h.tensor_sub(
                f4[:, :], f2[:, :], cPP[:, :]), inc="sv")
            k.op("vector", lambda h, jc=jc: h.scalar_tensor_tensor(
                cPP[:, :], f4[:, :], cm_t[:, jc:jc + 1], cPP[:, :],
                ALU.mult, ALU.add), inc="sv")
        k.autofence = False
        k.tile_barrier()
        G.close()

        B = Phase(nc)
        wo_s = B.sb("wo_s", [128, CH, D], BF16)
        cwr_s = B.sb("cwr_s", [128, CH, D], BF16)
        k.dma("gpsimd", wo_s[:, :, :],
              wo.rearrange("(kk p) e -> p kk e", p=128), "qg")
        k.dma("gpsimd", cwr_s[:, :, :],
              cwr.rearrange("(kk p) e -> p kk e", p=128), "qg")
        k.wait("tensor", "qg", k.now("qg"))

        NB = 256
        bK = B.sb("bK", [128, CH, NB], F32)
        bV = B.sb("bV", [128, CH, NB], F32)
        bR = B.sb("bR", [128, CH, NB], F32)
        bXN = B.sb("bXN", [128, CH, NB], F32)
        bPP = B.sb("bPP", [128, CH, NB], F32)
        bQ = B.sb("bQ", [128, CH, NB], F32)
        bD = B.sb("bD", [128, CH, NB], F32)
        bAA = B.sb("bAA", [128, CH, NB], F32)
        bBB = B.sb("bBB", [128, CH, NB], F32)
        bRW = B.sb("bRW", [128, CH, NB], BF16)
        bH2 = B.sb("bH2", [128, CH, NB], F32)
        bUCK = B.sb("bUCK", [128, CH, NB], BF16)
        bUCR = B.sb("bUCR", [128, CH, NB], BF16)
        bRRP = B.sb("bRRP", [128, 2, NB], F32)
        tm_buf = B.sb("tm_buf", [128, D], F32)
        tm_scr = B.sb("tm_scr", [128, D], F32)

        tiles = [(NB * t0, NB) for t0 in range(TOK // NB)] + [(TOK, 1)]
        for ti, (j0, n) in enumerate(tiles):
            k.autofence = (n < 128)
            slow = {"allow_slow_non_contiguous": True} if n == 1 else {}
            k.dma("sync", bK[:, :, 0:n], k_sp[:, :, j0:j0 + n], **slow)
            k.dma("sync", bV[:, :, 0:n], v_sp[:, :, j0:j0 + n], **slow)
            k.dma("sync", bR[:, :, 0:n], r_sp[:, :, j0:j0 + n], **slow)
            k.dma("sync", bXN[:, :, 0:n], xn_sp[:, :, j0:j0 + n], **slow)
            k.wait("vector", "qs", k.now("qs"))
            for c in range(CH):
                k.op("vector", lambda h, c=c, n=n: h.tensor_tensor_scan(
                    bPP[:, c, 0:n],
                    w_t[:, c:c + 1].broadcast_to([128, n]),
                    bK[:, c, 0:n], cPP[:, c:c + 1], ALU.add, ALU.max),
                    inc="sv")
            with k.af():
                k.op("vector", lambda h: h.tensor_sub(
                    bQ[:, :, 0:1], cPP.ap().unsqueeze(2), bPP[:, :, 0:1]),
                    inc="sv")
            if n > 1:
                k.op("vector", lambda h, n=n: h.tensor_sub(
                    bQ[:, :, 1:n], bPP[:, :, 0:n - 1], bPP[:, :, 1:n]),
                    inc="sv")
            k.op("vector", lambda h, n=n: h.tensor_sub(
                bK[:, :, 0:n], bK[:, :, 0:n], bPP[:, :, 0:n]), inc="sv")
            k.wait("scalar", "sv", k.now("sv"))
            for c in range(CH):
                k.op("scalar", lambda h, c=c, n=n: h.activation(
                    bD[:, c, 0:n], bQ[:, c, 0:n], AF.Exp,
                    bias=w_t[:, c:c + 1]), inc="ss")
            k.op("scalar", lambda h, n=n: h.activation(
                bQ[:, :, 0:n], bQ[:, :, 0:n], AF.Exp), inc="ss")
            k.op("scalar", lambda h, n=n: h.activation(
                bK[:, :, 0:n], bK[:, :, 0:n], AF.Exp), inc="ss")
            k.wait("vector", "ss", k.now("ss"))
            k.op("vector", lambda h, n=n: h.tensor_mul(
                bV[:, :, 0:n], bK[:, :, 0:n], bV[:, :, 0:n]), inc="sv")
            for c in range(CH):
                k.op("vector", lambda h, c=c, n=n: h.tensor_tensor_scan(
                    bAA[:, c, 0:n], bD[:, c, 0:n], bV[:, c, 0:n],
                    cAA[:, c:c + 1], ALU.mult, ALU.add), inc="sv")
                k.op("vector", lambda h, c=c, n=n: h.tensor_tensor_scan(
                    bBB[:, c, 0:n], bD[:, c, 0:n], bK[:, c, 0:n],
                    cBB[:, c:c + 1], ALU.mult, ALU.add), inc="sv")
            with k.af():
                k.op("vector", lambda h: h.tensor_mul(
                    bD[:, :, 0:1], bQ[:, :, 0:1], cAA.ap().unsqueeze(2)),
                    inc="sv")
            if n > 1:
                k.op("vector", lambda h, n=n: h.tensor_mul(
                    bD[:, :, 1:n], bQ[:, :, 1:n], bAA[:, :, 0:n - 1]), inc="sv")
            for c in range(CH):
                k.op("vector", lambda h, c=c, n=n: h.scalar_tensor_tensor(
                    bD[:, c, 0:n], bV[:, c, 0:n], U_t[:, c:c + 1],
                    bD[:, c, 0:n], ALU.mult, ALU.add), inc="sv")
            with k.af():
                k.op("vector", lambda h: h.tensor_mul(
                    bV[:, :, 0:1], bQ[:, :, 0:1], cBB.ap().unsqueeze(2)),
                    inc="sv")
            if n > 1:
                k.op("vector", lambda h, n=n: h.tensor_mul(
                    bV[:, :, 1:n], bQ[:, :, 1:n], bBB[:, :, 0:n - 1]), inc="sv")
            for c in range(CH):
                k.op("vector", lambda h, c=c, n=n: h.scalar_tensor_tensor(
                    bV[:, c, 0:n], bK[:, c, 0:n], U_t[:, c:c + 1],
                    bV[:, c, 0:n], ALU.mult, ALU.add), inc="sv")
            with k.af():
                k.op("vector", lambda h, n=n: h.tensor_copy(
                    cPP[:, :], bPP.ap()[:, :, n - 1]), inc="sv")
                k.op("vector", lambda h, n=n: h.tensor_copy(
                    cAA[:, :], bAA.ap()[:, :, n - 1]), inc="sv")
                k.op("vector", lambda h, n=n: h.tensor_copy(
                    cBB[:, :], bBB.ap()[:, :, n - 1]), inc="sv")
            k.op("vector", lambda h, n=n: h.tensor_scalar(
                bR[:, :, 0:n], bR[:, :, 0:n], -1.0, 20.0, ALU.mult, ALU.min),
                inc="sv")
            pre_er = k.now("sv")
            k.wait("scalar", "sv", pre_er)
            k.op("scalar", lambda h, n=n: h.activation(
                bR[:, :, 0:n], bR[:, :, 0:n], AF.Exp), inc="ss")
            k.wait("vector", "ss", k.now("ss"))
            k.op("vector", lambda h, n=n: h.tensor_mul(
                bK[:, :, 0:n], bV[:, :, 0:n], bR[:, :, 0:n]), inc="sv")
            k.op("vector", lambda h, n=n: h.tensor_add(
                bV[:, :, 0:n], bV[:, :, 0:n], bK[:, :, 0:n]), inc="sv")
            k.wait("scalar", "sv", k.now("sv"))
            k.op("scalar", lambda h, n=n: h.activation(
                bV[:, :, 0:n], bV[:, :, 0:n], AF.Ln), inc="ss")
            k.op("scalar", lambda h, n=n: h.activation(
                bV[:, :, 0:n], bV[:, :, 0:n], AF.Exp, scale=-1.0), inc="ss")
            k.wait("vector", "ss", k.now("ss"))
            k.op("vector", lambda h, n=n: h.tensor_mul(
                bRW[:, :, 0:n], bD[:, :, 0:n], bV[:, :, 0:n]), inc="sv")
            rw_done = k.now("sv")
            k.wait("tensor", "sv", rw_done)
            xm_ev = {0: 0, 1: 0}
            for o in range(CH):
                bank = ps_m[o % 2]
                k.wait("tensor", "sv", xm_ev[o % 2])
                for kt in range(CH):
                    k.op("tensor", lambda h, o=o, kt=kt, n=n, bank=bank:
                         h.matmul(bank[:, 0:n],
                                  wo_s[:, kt, 128 * o:128 * (o + 1)],
                                  bRW[:, kt, 0:n], start=(kt == 0),
                                  stop=(kt == 7)), inc="st")
                k.wait("vector", "st", k.now("st"))
                k.op("vector", lambda h, o=o, n=n, bank=bank: h.tensor_add(
                    bXN[:, o, 0:n], bXN[:, o, 0:n], bank[:, 0:n]), inc="sv")
                xm_ev[o % 2] = k.now("sv")
            xmid_done = k.now("sv")
            u0 = max(1, j0)
            cnt = j0 + n - u0
            off = u0 - j0
            slowc = {"allow_slow_non_contiguous": True} if cnt == 1 else {}
            k.wait("sync", "sv", xmid_done)
            k.dma("sync", xmid_sp[:, :, u0 - 1:u0 - 1 + cnt],
                  bXN.ap()[:, :, off:off + cnt], **slowc)
            nblk = (n + 127) // 128
            for tb in range(nblk):
                w = min(128, n - 128 * tb)
                k.wait("tensor", "sv", xmid_done)
                for c in range(CH):
                    k.op("tensor", lambda h, c=c, tb=tb, w=w: h.transpose(
                        ps_a[:w, 128 * c:128 * (c + 1)],
                        bXN[:, c, 128 * tb:128 * tb + w], id_t[:, :]),
                        inc="st")
                tp_done = k.now("st")
                k.wait("scalar", "st", tp_done)
                ln_tokmajor(k, ps_a[:w, :], tm_buf[:w, :], w, tm_scr,
                            s1, s2, st1, st2, st3)
                k.wait("tensor", "ss", k.now("ss"))
                for c in range(CH):
                    k.op("tensor", lambda h, c=c, w=w: h.transpose(
                        ps_b[:, 128 * c:128 * c + w],
                        tm_buf[:w, 128 * c:128 * (c + 1)], id_t[:w, :w]),
                        inc="st")
                k.wait("vector", "st", k.now("st"))
                for c in range(CH):
                    k.op("vector", lambda h, c=c, tb=tb, w=w: h.tensor_scalar(
                        bH2[:, c, 128 * tb:128 * tb + w],
                        ps_b[:, 128 * c:128 * c + w],
                        tabs["ln2_g"][:, c:c + 1], tabs["ln2_b"][:, c:c + 1],
                        ALU.mult, ALU.add), inc="sv")
                k.wait("tensor", "sv", k.now("sv"))
                k.wait("scalar", "sv", k.now("sv"))
            if ti == 0:
                with k.af():
                    k.op("vector", lambda h: h.tensor_scalar(
                        bH2[:, :, 0:1], bH2[:, :, 0:1], hm_t[:, 0:1], None,
                        ALU.mult), inc="sv")
            for mtn, ub in (("cmix_k", bUCK), ("cmix_r", bUCR)):
                mt = tabs[mtn]
                for c in range(CH):
                    if off == 0:
                        with k.af():
                            k.op("vector", lambda h, c=c, mt=mt, ub=ub: (
                                h.tensor_sub(bD[:, c, 0:1], bH2[:, c, 0:1],
                                             cH2.ap()[:, c:c + 1].unsqueeze(2))),
                                inc="sv")
                            k.op("vector", lambda h, c=c, mt=mt, ub=ub:
                                 h.scalar_tensor_tensor(
                                     ub[:, c, 0:1], bD[:, c, 0:1], mt[:, c:c + 1],
                                     cH2.ap()[:, c:c + 1].unsqueeze(2),
                                     ALU.mult, ALU.add), inc="sv")
                        if cnt > 1:
                            k.op("vector", lambda h, c=c, cnt=cnt: h.tensor_sub(
                                bD[:, c, 1:cnt], bH2[:, c, 1:cnt],
                                bH2[:, c, 0:cnt - 1]), inc="sv")
                            k.op("vector", lambda h, c=c, mt=mt, ub=ub, cnt=cnt:
                                 h.scalar_tensor_tensor(
                                     ub[:, c, 1:cnt], bD[:, c, 1:cnt],
                                     mt[:, c:c + 1], bH2[:, c, 0:cnt - 1],
                                     ALU.mult, ALU.add), inc="sv")
                    else:
                        k.op("vector", lambda h, c=c, cnt=cnt, off=off:
                             h.tensor_sub(
                                 bD[:, c, 0:cnt], bH2[:, c, off:off + cnt],
                                 bH2[:, c, off - 1:off - 1 + cnt]), inc="sv")
                        k.op("vector", lambda h, c=c, mt=mt, ub=ub, cnt=cnt,
                             off=off: h.scalar_tensor_tensor(
                                 ub[:, c, 0:cnt], bD[:, c, 0:cnt],
                                 mt[:, c:c + 1],
                                 bH2[:, c, off - 1:off - 1 + cnt],
                                 ALU.mult, ALU.add), inc="sv")
            mix_done = k.now("sv")
            with k.af():
                k.op("vector", lambda h, n=n: h.tensor_copy(
                    cH2[:, :], bH2.ap()[:, :, n - 1]), inc="sv")
            k.wait("sync", "sv", mix_done)
            k.dma("sync", uck_sp[:, :, u0 - 1:u0 - 1 + cnt],
                  bUCK.ap()[:, :, 0:cnt], **slowc)
            k.wait("tensor", "sv", mix_done)
            rr_ev = 0
            for o in range(CH):
                k.wait("tensor", "qs", rr_ev)
                for kt in range(CH):
                    k.op("tensor", lambda h, o=o, kt=kt, cnt=cnt: h.matmul(
                        ps_k[:, 0:cnt], cwr_s[:, kt, 128 * o:128 * (o + 1)],
                        bUCR[:, kt, 0:cnt], start=(kt == 0), stop=(kt == 7)),
                        inc="st")
                k.wait("vector", "st", k.now("st"))
                k.op("vector", lambda h, cnt=cnt: h.tensor_scalar(
                    tm_scr[:, 0:cnt], ps_k[:, 0:cnt], -1.0, 20.0,
                    ALU.mult, ALU.min), inc="sv")
                k.wait("scalar", "sv", k.now("sv"))
                k.op("scalar", lambda h, cnt=cnt: h.activation(
                    tm_scr[:, 0:cnt], tm_scr[:, 0:cnt], AF.Exp), inc="ss")
                k.op("scalar", lambda h, cnt=cnt: h.activation(
                    tm_scr[:, 0:cnt], tm_scr[:, 0:cnt], AF.Ln, bias=1.0),
                    inc="ss")
                slot = o % 2
                k.op("scalar", lambda h, cnt=cnt, slot=slot: h.activation(
                    bRRP[:, slot, 0:cnt], tm_scr[:, 0:cnt], AF.Exp,
                    scale=-1.0), inc="ss")
                k.wait("sync", "ss", k.now("ss"))
                k.dma("sync", rr_sp[:, o, u0 - 1:u0 - 1 + cnt],
                      bRRP.ap()[:, slot, 0:cnt], **slowc)
                rr_ev = k.now("qs")
            k.tile_barrier()
            k.autofence = False
        B.close()

        B2 = Phase(nc)
        cwk_s = B2.sb("cwk_s", [128, CH, FF], BF16)
        cwv_s = B2.sb("cwv_s", [128, FFT, D], BF16)
        k.dma("gpsimd", cwk_s[:, :, :],
              cwk.rearrange("(kk p) f -> p kk f", p=128), "qg")
        k.dma("gpsimd", cwv_s[:, :, :],
              cwv.rearrange("(kk p) e -> p kk e", p=128), "qg")
        k.wait("tensor", "qg", k.now("qg"))
        bUK2 = B2.sb("bUK2", [128, CH, 512], BF16)
        bRR2 = B2.sb("bRR2", [128, CH, 512], F32)
        kk_bf = B2.sb("kk_bf", [128, FFT, 512], BF16)
        scr2 = B2.sb("scr2", [128, 512], F32)
        a2ev = B2.sb("a2ev", [128, 2, 512], F32)
        a2_ev = {0: 0, 1: 0}
        for b2 in range(4):
            m0 = 512 * b2
            k.dma("sync", bUK2[:, :, :], uck_sp[:, :, m0:m0 + 512])
            k.dma("sync", bRR2[:, :, :], rr_sp[:, :, m0:m0 + 512])
            k.wait("tensor", "qs", k.now("qs"))
            for fo in range(FFT):
                for kt in range(CH):
                    k.op("tensor", lambda h, fo=fo, kt=kt: h.matmul(
                        ps_k[:, :], cwk_s[:, kt, 128 * fo:128 * (fo + 1)],
                        bUK2[:, kt, :], start=(kt == 0), stop=(kt == 7)),
                        inc="st")
                k.wait("vector", "st", k.now("st"))
                k.op("vector", lambda h: h.tensor_scalar(
                    scr2[:, :], ps_k[:, :], 0.0, None, ALU.max), inc="sv")
                k.wait("scalar", "sv", k.now("sv"))
                k.op("scalar", lambda h, fo=fo: h.activation(
                    kk_bf[:, fo, :], scr2[:, :], AF.Square), inc="ss")
                k.wait("tensor", "ss", k.now("ss"))
            for o in range(CH):
                bank = ps_m[o % 2]
                k.wait("tensor", "qs", a2_ev[o % 2])
                for kt in range(FFT):
                    k.op("tensor", lambda h, o=o, kt=kt, bank=bank: h.matmul(
                        bank[:, :], cwv_s[:, kt, 128 * o:128 * (o + 1)],
                        kk_bf[:, kt, :], start=(kt == 0), stop=(kt == FFT - 1)),
                        inc="st")
                k.wait("vector", "st", k.now("st"))
                slot = o % 2
                k.op("vector", lambda h, o=o, slot=slot, bank=bank:
                     h.tensor_mul(a2ev[:, slot, :], bRR2[:, o, :], bank[:, :]),
                     inc="sv")
                k.wait("sync", "sv", k.now("sv"))
                k.dma("sync", a2rr_sp[:, o, m0:m0 + 512],
                      a2ev.ap()[:, slot, :])
                a2_ev[slot] = k.now("qs")
            k.tile_barrier()
        B2.close()

        B3 = Phase(nc)
        bXM3 = B3.sb("bXM3", [128, CH, 512], F32)
        bA23 = B3.sb("bA23", [128, CH, 512], F32)
        bG3 = B3.sb("bG3", [128, CH, 512], F32)
        x_tm3 = B3.sb("x_tm3", [128, 4, D], F32)
        o_tm = B3.sb("o_tm", [128, D], F32)
        for b3 in range(4):
            m0 = 512 * b3
            k.dma("sync", bXM3[:, :, :], xmid_sp[:, :, m0:m0 + 512])
            k.dma("sync", bA23[:, :, :], a2rr_sp[:, :, m0:m0 + 512])
            k.dma("sync", bG3[:, :, :], g_sp[:, :, m0:m0 + 512])
            for b in range(4):
                k.dma("sync", x_tm3[:, b, :], x_b[m0 + 128 * b:m0 + 128 * (b + 1), :])
            k.wait("vector", "qs", k.now("qs"))
            k.op("vector", lambda h: h.tensor_add(
                bXM3[:, :, :], bXM3[:, :, :], bA23[:, :, :]), inc="sv")
            k.op("vector", lambda h: h.tensor_mul(
                bXM3[:, :, :], bXM3[:, :, :], bG3[:, :, :]), inc="sv")
            sdone = k.now("sv")
            k.wait("tensor", "sv", sdone)
            for tb in range(4):
                for c in range(CH):
                    k.op("tensor", lambda h, c=c, tb=tb: h.transpose(
                        ps_a[:, 128 * c:128 * (c + 1)],
                        bXM3[:, c, 128 * tb:128 * (tb + 1)], id_t[:, :]),
                        inc="st")
                k.wait("vector", "st", k.now("st"))
                k.op("vector", lambda h, tb=tb: h.scalar_tensor_tensor(
                    o_tm[:, :], ps_a[:, :], C127_3, x_tm3[:, tb, :],
                    ALU.mult, ALU.add), inc="sv")
                odone = k.now("sv")
                k.wait("sync", "sv", odone)
                k.dma("sync", out[m0 + 128 * tb:m0 + 128 * (tb + 1), :],
                      o_tm[:, :])
                k.wait("tensor", "qs", k.now("qs"))
                k.wait("vector", "qs", k.now("qs"))
            k.tile_barrier()
        B3.close()

        k.wait("sync", "qs", k.now("qs"))
        k.wait("sync", "qg", k.now("qg"))
        block = ctx.enter_context(nc.Block())
        k.emit(block)
    return nc


def _vec_layout(v):
    return np.ascontiguousarray(np.asarray(v, np.float32).reshape(CH, 128).T)


def kernel(L0_attn_x, L0_attn_ln1_g, L0_attn_ln1_b, L0_attn_ln2_g,
           L0_attn_ln2_b, L0_attn_tm_mix_k, L0_attn_tm_mix_v, L0_attn_tm_mix_r,
           L0_attn_time_decay, L0_attn_time_first, L0_attn_wk, L0_attn_wv,
           L0_attn_wr, L0_attn_wo, L0_attn_cm_mix_k, L0_attn_cm_mix_r,
           L0_attn_cwk, L0_attn_cwv, L0_attn_cwr, L0_ffn_w_gate, L0_ffn_w_up,
           L0_ffn_w_down):
    x = np.ascontiguousarray(np.asarray(L0_attn_x, np.float32).reshape(T, D))
    f32 = lambda a: np.ascontiguousarray(np.asarray(a, np.float32))
    i32 = lambda a: np.ascontiguousarray(np.asarray(a, np.int32))
    ident = np.eye(128, dtype=np.float32)
    ramp = np.tile(np.arange(128, dtype=np.float32), (128, 1))
    ramp = np.ascontiguousarray(ramp)

    common1 = {
        "wk": f32(L0_attn_wk), "wv": f32(L0_attn_wv), "wr": f32(L0_attn_wr),
        "wg_i": i32(L0_ffn_w_gate), "wu_i": i32(L0_ffn_w_up),
        "wd_i": i32(L0_ffn_w_down),
        "ln1_g": _vec_layout(L0_attn_ln1_g), "ln1_b": _vec_layout(L0_attn_ln1_b),
        "mix_k": _vec_layout(L0_attn_tm_mix_k),
        "mix_v": _vec_layout(L0_attn_tm_mix_v),
        "mix_r": _vec_layout(L0_attn_tm_mix_r),
        "tdec": _vec_layout(L0_attn_time_decay),
        "ident": ident, "ramp": ramp,
    }
    maps1 = []
    for i in range(NCORES):
        s = TOK * i
        xa = np.zeros((TOK + 2, D), np.float32)
        if i == 0:
            xa[2:] = x[0:TOK]
        else:
            xa[:] = x[s - 2:s + TOK]
        hm = np.zeros((128, 1), np.float32) if i == 0 else \
            np.ones((128, 1), np.float32)
        kb = np.full((128, 1), -1.0e30, np.float32) if i == 0 else \
            np.zeros((128, 1), np.float32)
        maps1.append({**common1, "x_a1": xa, "hmask": hm, "kbias": kb})

    nc1 = build_launch1()
    res1 = run_bass_kernel_spmd(nc1, maps1, core_ids=list(range(NCORES)))

    summ_all = np.stack([np.asarray(res1.results[i]["summ"])
                         for i in range(NCORES)], axis=0)
    summ_all = np.ascontiguousarray(summ_all.transpose(2, 0, 1, 3))

    common2 = {
        "wo": f32(L0_attn_wo), "cwr": f32(L0_attn_cwr),
        "cwk": f32(L0_attn_cwk), "cwv": f32(L0_attn_cwv),
        "ln2_g": _vec_layout(L0_attn_ln2_g), "ln2_b": _vec_layout(L0_attn_ln2_b),
        "cmix_k": _vec_layout(L0_attn_cm_mix_k),
        "cmix_r": _vec_layout(L0_attn_cm_mix_r),
        "tdec": _vec_layout(L0_attn_time_decay),
        "tfirst": _vec_layout(L0_attn_time_first),
        "ident": ident, "summ_all": summ_all,
    }
    maps2 = []
    for i in range(NCORES):
        s = TOK * i
        hm = np.zeros((128, 1), np.float32) if i == 0 else \
            np.ones((128, 1), np.float32)
        cm = np.zeros((128, NCORES), np.float32)
        cm[:, :i] = 1.0
        r1 = res1.results[i]
        maps2.append({**common2, "x_b": np.ascontiguousarray(x[s:s + TOK]),
                      "hmask": hm, "cmask": cm,
                      "xn_sp": np.asarray(r1["xn_sp"]),
                      "k_sp": np.asarray(r1["k_sp"]),
                      "v_sp": np.asarray(r1["v_sp"]),
                      "r_sp": np.asarray(r1["r_sp"]),
                      "g_sp": np.asarray(r1["g_sp"])})

    nc2 = build_launch2()
    res2 = run_bass_kernel_spmd(nc2, maps2, core_ids=list(range(NCORES)))
    out = np.concatenate([np.asarray(res2.results[i]["out"])
                          for i in range(NCORES)], axis=0)
    return out.reshape(1, T, D)


# version 03
# speedup vs baseline: 1.1941x; 1.1941x over previous
import numpy as np
import concourse.bass as bass
import concourse.mybir as mybir
from concourse.bass_utils import run_bass_kernel_spmd
from contextlib import ExitStack

F32 = mybir.dt.float32
BF16 = mybir.dt.bfloat16
I32 = mybir.dt.int32
AF = mybir.ActivationFunctionType
ALU = mybir.AluOpType
AX = mybir.AxisListType.X

NCORES = 8
T = 16384
D = 1024
CH = 8
FF = 4096
FH = 3072
FHT = FH // 128
FFT = FF // 128
TOK = T // NCORES
J = TOK + 1
NEG = -1.0e38
EPS = 1e-5
C127_3 = float(1.0 / (127.0 ** 3))


class K:
    ENGINES = ("sync", "gpsimd", "vector", "scalar", "tensor")

    PROG = {"vector": "sv", "scalar": "ss", "gpsimd": "sg", "tensor": "st"}

    def __init__(self, nc, ctx=None):
        self.nc = nc
        self.ctx = ctx
        self.streams = {e: [] for e in self.ENGINES}
        self.led = {}
        self.sems = {}
        self.autofence = False

    def sem(self, ctx, name):
        self.sems[name] = ctx.enter_context(self.nc.semaphore(name))
        self.led[name] = 0

    def ensure(self, name):
        if name not in self.sems:
            self.sem(self.ctx, name)
        return name

    def op(self, eng, fn, inc=None, n=1):
        fence = self.autofence and eng in ("vector", "scalar")
        if fence and inc is None:
            inc = self.PROG[eng]
        if inc is not None:
            self.led[inc] += n
        self.streams[eng].append(("i", fn, inc, n))
        if fence:
            self.wait(eng, inc, self.led[inc])
        return self.led[inc] if inc else None

    from contextlib import contextmanager as _cm

    @_cm
    def af(self):
        prev = self.autofence
        self.autofence = True
        try:
            yield
        finally:
            self.autofence = prev

    def dma(self, eng, out, in_, inc="qs", **kw):
        self.ensure(inc)
        fn = (lambda out=out, in_=in_, kw=kw:
              (lambda h: h.dma_start(out=out, in_=in_, **kw)))()
        return self.op(eng, fn, inc=inc, n=16)

    def wait_s(self, eng, sem, val):
        self.ensure(sem)
        self.wait(eng, sem, val)

    def wait(self, eng, sem, val):
        if val > 0:
            self.streams[eng].append(("w", sem, val))

    def now(self, sem):
        self.ensure(sem)
        return self.led[sem]

    def tile_barrier(self, sems=("qs", "sv", "ss", "st")):
        counts = [(s, self.led[s]) for s in sems]
        for eng in self.ENGINES:
            for s, v in counts:
                self.wait(eng, s, v)

    def emit(self, block):
        def mk(eng_name):
            stream = self.streams[eng_name]
            sems = self.sems

            def body(h):
                for item in stream:
                    if item[0] == "i":
                        _, fn, inc, n = item
                        ins = fn(h)
                        if inc is not None:
                            ins.then_inc(sems[inc], n)
                    else:
                        _, sem, val = item
                        h.wait_ge(sems[sem], val)
            return body
        block.sync(mk("sync"))
        block.gpsimd(mk("gpsimd"))
        block.vector(mk("vector"))
        block.scalar(mk("scalar"))
        block.tensor(mk("tensor"))


class Phase:

    def __init__(self, nc):
        self.nc = nc
        self.stack = ExitStack()

    def sb(self, name, shape, dt):
        return self.stack.enter_context(self.nc.sbuf_tensor(name, shape, dt))

    def close(self):
        self.stack.close()


def ln_tokmajor(k, blk_in, blk_out, npart, scr, s1, s2, st1, st2, st3,
                epsv):
    with k.af():
        k.op("scalar", lambda h: h.activation(
            scr[:npart, :], blk_in, AF.Square, accum_out=s2[:npart, :]),
            inc="ss")
        k.op("scalar", lambda h: h.activation(
            scr[:npart, :], blk_in, AF.Identity, accum_out=s1[:npart, :]),
            inc="ss")
        k.op("scalar", lambda h: h.mul(st1[:npart, :], s1[:npart, :], 1.0 / D),
             inc="ss")
        k.op("scalar", lambda h: h.activation(
            st2[:npart, :], st1[:npart, :], AF.Square), inc="ss")
        k.op("scalar", lambda h: h.activation(
            st2[:npart, :], st2[:npart, :], AF.Identity, scale=-1.0,
            bias=epsv[:npart, :]), inc="ss")
        k.op("scalar", lambda h: h.activation(
            st2[:npart, :], s2[:npart, :], AF.Identity, scale=1.0 / D,
            bias=st2[:npart, 0:1]), inc="ss")
        k.op("scalar", lambda h: h.activation(
            st2[:npart, :], st2[:npart, :], AF.Ln), inc="ss")
        k.op("scalar", lambda h: h.activation(
            st2[:npart, :], st2[:npart, :], AF.Exp, scale=-0.5), inc="ss")
        k.op("scalar", lambda h: h.mul(st3[:npart, :], st1[:npart, :],
                                       st2[:npart, 0:1]), inc="ss")
        k.op("scalar", lambda h: h.mul(st3[:npart, :], st3[:npart, :], -1.0),
             inc="ss")
        k.op("scalar", lambda h: h.activation(
            blk_out, blk_in, AF.Identity, bias=st3[:npart, 0:1],
            scale=st2[:npart, 0:1]), inc="ss")
    return k.now("ss")


def build_launch1():
    nc = bass.Bass("TRN2", target_bir_lowering=False)
    x_a1 = nc.dram_tensor("x_a1", [TOK + 2, D], F32, kind="ExternalInput").ap()
    wk = nc.dram_tensor("wk", [D, D], F32, kind="ExternalInput").ap()
    wv = nc.dram_tensor("wv", [D, D], F32, kind="ExternalInput").ap()
    wr = nc.dram_tensor("wr", [D, D], F32, kind="ExternalInput").ap()
    wg_i = nc.dram_tensor("wg_i", [FH, D], I32, kind="ExternalInput").ap()
    wu_i = nc.dram_tensor("wu_i", [FH, D], I32, kind="ExternalInput").ap()
    wd_i = nc.dram_tensor("wd_i", [D, FH], I32, kind="ExternalInput").ap()
    vecs = {}
    for nm in ("ln1_g", "ln1_b", "mix_k", "mix_v", "mix_r", "tdec"):
        vecs[nm] = nc.dram_tensor(nm, [128, CH], F32, kind="ExternalInput").ap()
    hmask = nc.dram_tensor("hmask", [128, 1], F32, kind="ExternalInput").ap()
    kbias = nc.dram_tensor("kbias", [128, 1], F32, kind="ExternalInput").ap()
    epsin = nc.dram_tensor("epsin", [128, 1], F32, kind="ExternalInput").ap()
    ident = nc.dram_tensor("ident", [128, 128], F32, kind="ExternalInput").ap()
    ramp = nc.dram_tensor("ramp", [128, 128], F32, kind="ExternalInput").ap()

    xn_sp = nc.dram_tensor("xn_sp", [128, CH, J], F32, kind="ExternalOutput").ap()
    k_sp = nc.dram_tensor("k_sp", [128, CH, J], F32, kind="ExternalOutput").ap()
    v_sp = nc.dram_tensor("v_sp", [128, CH, J], F32, kind="ExternalOutput").ap()
    r_sp = nc.dram_tensor("r_sp", [128, CH, J], F32, kind="ExternalOutput").ap()
    g_sp = nc.dram_tensor("g_sp", [128, CH, TOK], F32, kind="ExternalOutput").ap()
    summ = nc.dram_tensor("summ", [3, 128, CH], F32, kind="ExternalOutput").ap()
    wg_b = nc.dram_tensor("wg_b", [FH, D], BF16, kind="Internal").ap()
    wu_b = nc.dram_tensor("wu_b", [FH, D], BF16, kind="Internal").ap()
    wd_b = nc.dram_tensor("wd_b", [D, FH], BF16, kind="Internal").ap()

    with ExitStack() as ctx:
        k = K(nc, ctx)
        for s in ("qs", "qg", "sv", "ss", "st", "sg"):
            k.sem(ctx, s)
        sb = lambda nm, shape, dt: ctx.enter_context(nc.sbuf_tensor(nm, shape, dt))
        psb = lambda nm, w: ctx.enter_context(nc.psum_tensor(nm, [128, w], F32))

        tabs = {}
        for nm, ap_ in vecs.items():
            tabs[nm] = sb(nm + "_t", [128, CH], F32)
            k.dma("sync", tabs[nm][:, :], ap_, "qt")
        negw = sb("negw", [128, CH], F32)
        hm_t = sb("hm_t", [128, 1], F32)
        kb_t = sb("kb_t", [128, 1], F32)
        eps_t = sb("eps_t", [128, 1], F32)
        id_t = sb("id_t", [128, 128], F32)
        ramp_t = sb("ramp_t", [128, 128], F32)
        k.dma("sync", hm_t[:, :], hmask, "qt")
        k.dma("sync", kb_t[:, :], kbias, "qt")
        k.dma("sync", eps_t[:, :], epsin, "qt")
        k.dma("sync", id_t[:, :], ident, "qt")
        k.dma("sync", ramp_t[:, :], ramp, "qt")
        s1 = sb("s1", [128, 1], F32)
        s2 = sb("s2", [128, 1], F32)
        st1 = sb("st1", [128, 1], F32)
        st2 = sb("st2", [128, 1], F32)
        st3 = sb("st3", [128, 1], F32)
        h1_carry = sb("h1_carry", [128, CH], F32)
        ps_a = psb("ps_a", D)
        ps_b = psb("ps_b", D)
        ps_m = [psb(f"ps_m{i}", 512) for i in range(2)]

        k.wait_s("scalar", "qt", k.now("qt"))
        k.op("scalar", lambda h: h.activation(
            negw[:, :], tabs["tdec"][:, :], AF.Exp), inc="ss")

        k.dma("gpsimd", wg_b, wg_i, "qgc")
        k.dma("gpsimd", wu_b, wu_i, "qgc")
        k.dma("gpsimd", wd_b, wd_i, "qgc")
        ffn_cast_done = k.now("qgc")

        A = Phase(nc)
        wk_s = A.sb("wk_s", [128, CH, D], BF16)
        wv_s = A.sb("wv_s", [128, CH, D], BF16)
        wr_s = A.sb("wr_s", [128, CH, D], BF16)
        for w_hbm, w_sb in ((wk, wk_s), (wv, wv_s), (wr, wr_s)):
            k.dma("gpsimd", w_sb[:, :, :],
                  w_hbm.rearrange("(kk p) e -> p kk e", p=128), "qga")
        a1_w_loaded = k.now("qga")
        k.wait("tensor", "qga", a1_w_loaded)

        x_tm = A.sb("x_tm", [128, 4, D], F32)
        t_tm = A.sb("t_tm", [128, D], F32)
        scr_tm = A.sb("scr_tm", [128, D], F32)
        xn_cm = A.sb("xn_cm", [128, CH, 512], F32)
        h1_cm = A.sb("h1_cm", [128, CH, 512], F32)
        dh = A.sb("dh", [128, CH, 512], F32)
        u_b = [A.sb(f"u_b{i}", [128, CH, 512], BF16) for i in range(3)]
        out_cm = {"k": A.sb("kcm", [128, CH, 512], F32),
                  "v": A.sb("vcm", [128, CH, 512], F32),
                  "r": A.sb("rcm", [128, CH, 512], F32)}
        spill_t = {"k": k_sp, "v": v_sp, "r": r_sp}

        tiles = [(512 * t0, 512) for t0 in range(4)] + [(TOK, 2)]
        bank_ev = {0: 0, 1: 0}
        mm_ev = {0: 0, 1: 0, 2: 0}
        sp_ev = {"xn": 0, "k": 0, "v": 0, "r": 0}
        ln_last = 0
        tp_last = 0
        evac_last = 0
        for ti, (tl0, n) in enumerate(tiles):
            nb = (n + 127) // 128
            if ti:
                k.wait("sync", "ss", ln_last)
                k.wait("sync", "st", tp_last)
            for b in range(nb):
                npart = min(128, n - 128 * b)
                k.dma("sync", x_tm[:npart, b, :],
                      x_a1[tl0 + 128 * b: tl0 + 128 * b + npart, :], "qx")
            k.wait_s("scalar", "qx", k.now("qx"))
            for b in range(nb):
                npart = min(128, n - 128 * b)
                blk = x_tm[:npart, b, :]
                ln_tokmajor(k, blk, blk, npart, scr_tm, s1, s2, st1, st2,
                            st3, eps_t)
                k.wait("scalar", "st", tp_last)
                xn_done = ln_tokmajor(k, blk, t_tm[:npart, :], npart,
                                      scr_tm, s1, s2, st1, st2, st3, eps_t)
                k.wait("tensor", "ss", xn_done)
                k.wait("tensor", "sv", evac_last)
                for c in range(CH):
                    k.op("tensor", lambda h, c=c, npart=npart, blk=blk: h.transpose(
                        ps_a[:, 128 * c:128 * c + npart],
                        blk[:, 128 * c:128 * (c + 1)], id_t[:npart, :npart]),
                        inc="st")
                for c in range(CH):
                    k.op("tensor", lambda h, c=c, npart=npart: h.transpose(
                        ps_b[:, 128 * c:128 * c + npart],
                        t_tm[:npart, 128 * c:128 * (c + 1)], id_t[:npart, :npart]),
                        inc="st")
                tp_last = k.now("st")
                k.wait("vector", "st", tp_last)
                if b == 0:
                    k.wait_s("vector", "qsxn", sp_ev["xn"])
                ps_a_v = ps_a.ap().rearrange("p (c q) -> p c q", c=CH)[:, :, :npart]
                k.op("vector", lambda h, b=b, npart=npart, s=ps_a_v: h.tensor_copy(
                    xn_cm.ap()[:, :, 128 * b:128 * b + npart], s), inc="sv")
                for c in range(CH):
                    k.op("vector", lambda h, c=c, b=b, npart=npart: h.tensor_scalar(
                        h1_cm[:, c, 128 * b:128 * b + npart],
                        ps_b[:, 128 * c:128 * c + npart],
                        tabs["ln1_g"][:, c:c + 1], tabs["ln1_b"][:, c:c + 1],
                        ALU.mult, ALU.add), inc="sv")
                evac_last = k.now("sv")
            ln_last = k.now("ss")
            if ti == 0:
                with k.af():
                    k.op("vector", lambda h: h.tensor_scalar(
                        h1_cm[:, :, 1:2], h1_cm[:, :, 1:2], hm_t[:, 0:1], None,
                        ALU.mult), inc="sv")
            k.wait("sync", "sv", k.now("sv"))
            if tl0 == 0:
                k.dma("sync", xn_sp[:, :, 0:n - 1], xn_cm.ap()[:, :, 1:n],
                      "qsxn")
            else:
                k.dma("sync", xn_sp[:, :, tl0 - 1:tl0 - 1 + n],
                      xn_cm.ap()[:, :, 0:n], "qsxn")
            sp_ev["xn"] = k.now("qsxn")

            u0 = max(1, tl0)
            cnt = tl0 + n - u0
            off = u0 - tl0
            j0 = u0 - 1
            if off == 0:
                with k.af():
                    k.op("vector", lambda h: h.tensor_sub(
                        dh[:, :, 0:1], h1_cm[:, :, 0:1],
                        h1_carry.ap().unsqueeze(2)), inc="sv")
                if cnt > 1:
                    k.op("vector", lambda h, cnt=cnt: h.tensor_sub(
                        dh[:, :, 1:cnt], h1_cm[:, :, 1:cnt],
                        h1_cm[:, :, 0:cnt - 1]), inc="sv")
            else:
                k.op("vector", lambda h, off=off, cnt=cnt: h.tensor_sub(
                    dh[:, :, 0:cnt], h1_cm[:, :, off:off + cnt],
                    h1_cm[:, :, off - 1:off - 1 + cnt]), inc="sv")

            for mi, (mtn, w_sb, spname) in enumerate((
                    ("mix_k", wk_s, "k"), ("mix_v", wv_s, "v"),
                    ("mix_r", wr_s, "r"))):
                mt = tabs[mtn]
                ub = u_b[mi]
                ocm = out_cm[spname]
                k.wait("vector", "st", mm_ev[mi])
                for c in range(CH):
                    if off == 0:
                        with k.af():
                            k.op("vector", lambda h, c=c, mt=mt, ub=ub:
                                 h.scalar_tensor_tensor(
                                     ub[:, c, 0:1], dh[:, c, 0:1], mt[:, c:c + 1],
                                     h1_carry.ap()[:, c:c + 1].unsqueeze(2),
                                     ALU.mult, ALU.add), inc="sv")
                        if cnt > 1:
                            k.op("vector", lambda h, c=c, mt=mt, ub=ub, cnt=cnt:
                                 h.scalar_tensor_tensor(
                                     ub[:, c, 1:cnt], dh[:, c, 1:cnt],
                                     mt[:, c:c + 1], h1_cm[:, c, 0:cnt - 1],
                                     ALU.mult, ALU.add), inc="sv")
                    else:
                        k.op("vector", lambda h, c=c, mt=mt, ub=ub, off=off,
                             cnt=cnt: h.scalar_tensor_tensor(
                                 ub[:, c, 0:cnt], dh[:, c, 0:cnt],
                                 mt[:, c:c + 1],
                                 h1_cm[:, c, off - 1:off - 1 + cnt],
                                 ALU.mult, ALU.add), inc="sv")
                mix_done = k.now("sv")
                k.wait("tensor", "sv", mix_done)
                for o in range(CH):
                    bank = ps_m[o % 2]
                    k.wait("tensor", "sv", bank_ev[o % 2])
                    for kt in range(CH):
                        k.op("tensor", lambda h, o=o, kt=kt, w_sb=w_sb, ub=ub,
                             cnt=cnt, bank=bank: h.matmul(
                                 bank[:, 0:cnt],
                                 w_sb[:, kt, 128 * o:128 * (o + 1)],
                                 ub[:, kt, 0:cnt],
                                 start=(kt == 0), stop=(kt == 7)), inc="st")
                    k.wait("vector", "st", k.now("st"))
                    if o == 0:
                        k.wait_s("vector", "qs" + spname, sp_ev[spname])
                    k.op("vector", lambda h, o=o, ocm=ocm, cnt=cnt, bank=bank:
                         h.tensor_copy(ocm[:, o, 0:cnt], bank[:, 0:cnt]),
                         inc="sv")
                    bank_ev[o % 2] = k.now("sv")
                mm_ev[mi] = k.now("st")
                last_evac = k.now("sv")
                if spname == "k" and ti == 0:
                    with k.af():
                        k.op("vector", lambda h: h.tensor_scalar(
                            out_cm["k"][:, :, 0:1], out_cm["k"][:, :, 0:1],
                            hm_t[:, 0:1], kb_t[:, 0:1], ALU.mult, ALU.add),
                            inc="sv")
                    last_evac = k.now("sv")
                k.wait("sync", "sv", last_evac)
                k.dma("sync", spill_t[spname][:, :, j0:j0 + cnt],
                      ocm.ap()[:, :, 0:cnt], "qs" + spname)
                sp_ev[spname] = k.now("qs" + spname)
            with k.af():
                k.op("vector", lambda h, n=n: h.tensor_copy(
                    h1_carry[:, :], h1_cm.ap()[:, :, n - 1]), inc="sv")
        A.close()

        S = Phase(nc)
        sA = S.sb("sA", [128, CH], F32)
        sB = S.sb("sB", [128, CH], F32)
        sP = S.sb("sP", [128, CH], F32)
        tA = S.sb("tA", [128, CH], F32)
        tB = S.sb("tB", [128, CH], F32)
        tP = S.sb("tP", [128, CH], F32)
        f1 = S.sb("f1", [128, CH], F32)
        f2 = S.sb("f2", [128, CH], F32)
        m_buf = S.sb("m_buf", [128, CH, 128], F32)
        e_buf = S.sb("e_buf", [128, CH, 128], F32)
        kg_b = S.sb("kg_b", [128, CH, 128], F32)
        vg_b = S.sb("vg_b", [128, CH, 128], F32)
        negM = S.sb("negM", [128, CH], F32)
        M_b = S.sb("M_b", [128, CH], F32)
        k.wait("sync", "sv", k.now("sv"))
        k.wait("sync", "ss", k.now("ss"))
        k.wait("sync", "st", k.now("st"))
        k.autofence = True
        k.op("vector", lambda h: h.memset(sA[:, :], 0.0), inc="sv")
        k.op("vector", lambda h: h.memset(sB[:, :], 0.0), inc="sv")
        k.op("vector", lambda h: h.memset(sP[:, :], NEG), inc="sv")
        for g in range(16):
            if g:
                k.wait("sync", "sv", k.now("sv"))
                k.wait("sync", "ss", k.now("ss"))
            k.dma("sync", kg_b[:, :, :], k_sp[:, :, 128 * g:128 * (g + 1)],
                  "qsm")
            k.dma("sync", vg_b[:, :, :], v_sp[:, :, 128 * g:128 * (g + 1)],
                  "qsm")
            k.wait_s("vector", "qsm", k.now("qsm"))
            for c in range(CH):
                k.op("vector", lambda h, c=c: h.scalar_tensor_tensor(
                    m_buf[:, c, :], ramp_t[:, :], negw[:, c:c + 1],
                    kg_b[:, c, :], ALU.mult, ALU.add), inc="sv")
                k.op("vector", lambda h, c=c: h.reduce_max(
                    M_b[:, c:c + 1], m_buf[:, c, :], axis=AX), inc="sv")
            k.op("vector", lambda h: h.tensor_scalar(
                negM[:, :], M_b[:, :], -1.0, None, ALU.mult), inc="sv")
            k.wait("scalar", "sv", k.now("sv"))
            for c in range(CH):
                k.op("scalar", lambda h, c=c: h.activation(
                    e_buf[:, c, :], m_buf[:, c, :], AF.Exp,
                    bias=negM[:, c:c + 1], accum_out=tB[:, c:c + 1]), inc="ss")
            k.wait("vector", "ss", k.now("ss"))
            for c in range(CH):
                k.op("vector", lambda h, c=c: h.tensor_mul(
                    e_buf[:, c, :], e_buf[:, c, :], vg_b[:, c, :]), inc="sv")
                k.op("vector", lambda h, c=c: h.reduce_sum(
                    tA[:, c:c + 1], e_buf[:, c, :], axis=AX), inc="sv")
            k.op("vector", lambda h: h.scalar_tensor_tensor(
                tP[:, :], negw[:, :], 127.0, M_b[:, :], ALU.mult,
                ALU.subtract), inc="sv")
            k.op("vector", lambda h: h.tensor_scalar(
                tP[:, :], tP[:, :], -1.0, None, ALU.mult), inc="sv")
            k.op("vector", lambda h: h.scalar_tensor_tensor(
                f1[:, :], negw[:, :], -128.0, sP[:, :], ALU.mult, ALU.add),
                inc="sv")
            k.op("vector", lambda h: h.tensor_max(
                f2[:, :], f1[:, :], tP[:, :]), inc="sv")
            k.op("vector", lambda h: h.tensor_sub(f1[:, :], f1[:, :], f2[:, :]),
                 inc="sv")
            k.op("vector", lambda h: h.tensor_sub(tP[:, :], tP[:, :], f2[:, :]),
                 inc="sv")
            k.wait("scalar", "sv", k.now("sv"))
            k.op("scalar", lambda h: h.activation(f1[:, :], f1[:, :], AF.Exp),
                 inc="ss")
            k.op("scalar", lambda h: h.activation(tP[:, :], tP[:, :], AF.Exp),
                 inc="ss")
            k.wait("vector", "ss", k.now("ss"))
            for acc, tnew in ((sA, tA), (sB, tB)):
                k.op("vector", lambda h, acc=acc: h.tensor_mul(
                    acc[:, :], acc[:, :], f1[:, :]), inc="sv")
                k.op("vector", lambda h, tnew=tnew: h.tensor_mul(
                    tnew[:, :], tnew[:, :], tP[:, :]), inc="sv")
                k.op("vector", lambda h, acc=acc, tnew=tnew: h.tensor_add(
                    acc[:, :], acc[:, :], tnew[:, :]), inc="sv")
            k.op("vector", lambda h: h.tensor_copy(sP[:, :], f2[:, :]), inc="sv")
        k.autofence = False
        k.wait("sync", "sv", k.now("sv"))
        k.dma("sync", summ[0, :, :], sA[:, :], "qsum")
        k.dma("sync", summ[1, :, :], sB[:, :], "qsum")
        k.dma("sync", summ[2, :, :], sP[:, :], "qsum")
        S.close()

        F = Phase(nc)
        wgT = F.sb("wgT", [128, CH, FH], BF16)
        wuT = F.sb("wuT", [128, CH, FH], BF16)
        wdT = F.sb("wdT", [128, FHT, D], BF16)
        k.wait_s("sync", "qgc", ffn_cast_done)
        k.wait("sync", "sv", k.now("sv"))
        k.wait("sync", "ss", k.now("ss"))
        k.ensure("qw")
        for c in range(CH):
            k.op("sync", lambda h, c=c: h.dma_start(
                out=wgT[:, c, :], in_=wg_b[:, 128 * c:128 * (c + 1)],
                transpose=True), inc="qw", n=16)
            k.op("sync", lambda h, c=c: h.dma_start(
                out=wuT[:, c, :], in_=wu_b[:, 128 * c:128 * (c + 1)],
                transpose=True), inc="qw", n=16)
        for hc in range(FHT):
            k.op("sync", lambda h, hc=hc: h.dma_start(
                out=wdT[:, hc, :], in_=wd_b[:, 128 * hc:128 * (hc + 1)],
                transpose=True), inc="qw", n=16)
        ffn_w_loaded = k.now("qw")
        k.wait("tensor", "qw", ffn_w_loaded)
        k.wait("gpsimd", "qw", ffn_w_loaded)
        k.wait_s("gpsimd", "qsxn", k.now("qsxn"))

        xn_bf = F.sb("xn_bf", [128, CH, 512], BF16)
        hh_bf = F.sb("hh_bf", [128, FHT, 512], BF16)
        sg_scr = F.sb("sg_scr", [128, 512], F32)
        g_ev = F.sb("g_ev", [128, 2, 512], F32)
        ps_g = psb("ps_g", 512)
        ps_u = psb("ps_u", 512)
        gu_banks = [(ps_g.ap()[:, :], ps_u.ap()[:, :]),
                    (ps_a.ap()[:, 0:512], ps_a.ap()[:, 512:1024])]
        gev_ev = {0: 0, 1: 0}
        gmul_ev = {0: 0, 1: 0}
        gu_ev = {0: 0, 1: 0}
        a2mm_prev = 0
        k.wait("tensor", "sv", k.now("sv"))
        for b2 in range(4):
            m0 = 512 * b2
            k.wait("gpsimd", "st", a2mm_prev)
            k.dma("gpsimd", xn_bf[:, :, :], xn_sp[:, :, m0 + 1:m0 + 513],
                  "qgx")
            k.wait_s("tensor", "qgx", k.now("qgx"))
            for ho in range(FHT):
                bg, bu = gu_banks[ho % 2]
                k.wait("tensor", "sv", gu_ev[ho % 2])
                for kt in range(CH):
                    k.op("tensor", lambda h, ho=ho, kt=kt, bg=bg: h.matmul(
                        bg, wgT[:, kt, 128 * ho:128 * (ho + 1)],
                        xn_bf[:, kt, :], start=(kt == 0), stop=(kt == 7)),
                        inc="st")
                g_mm = k.now("st")
                for kt in range(CH):
                    k.op("tensor", lambda h, ho=ho, kt=kt, bu=bu: h.matmul(
                        bu, wuT[:, kt, 128 * ho:128 * (ho + 1)],
                        xn_bf[:, kt, :], start=(kt == 0), stop=(kt == 7)),
                        inc="st")
                u_mm = k.now("st")
                k.wait("vector", "st", g_mm)
                k.op("vector", lambda h, bg=bg: h.tensor_scalar(
                    sg_scr[:, :], bg, -1.0 / 127.0, 25.0,
                    ALU.mult, ALU.min), inc="sv")
                k.wait("scalar", "sv", k.now("sv"))
                k.op("scalar", lambda h: h.activation(
                    sg_scr[:, :], sg_scr[:, :], AF.Exp), inc="ss")
                k.op("scalar", lambda h: h.activation(
                    sg_scr[:, :], sg_scr[:, :], AF.Ln, bias=1.0), inc="ss")
                k.op("scalar", lambda h: h.activation(
                    sg_scr[:, :], sg_scr[:, :], AF.Exp, scale=-1.0), inc="ss")
                k.wait("vector", "ss", k.now("ss"))
                k.wait("vector", "st", u_mm)
                k.op("vector", lambda h, bg=bg: h.tensor_mul(
                    sg_scr[:, :], sg_scr[:, :], bg), inc="sv")
                k.op("vector", lambda h, ho=ho, bu=bu: h.tensor_mul(
                    hh_bf[:, ho, :], sg_scr[:, :], bu), inc="sv")
                gu_ev[ho % 2] = k.now("sv")
            hh_done = k.now("sv")
            k.wait("tensor", "sv", hh_done)
            for do in range(CH):
                bank = ps_m[do % 2]
                k.wait("tensor", "sv", gmul_ev[do % 2])
                for kt in range(FHT):
                    k.op("tensor", lambda h, do=do, kt=kt, bank=bank: h.matmul(
                        bank[:, :], wdT[:, kt, 128 * do:128 * (do + 1)],
                        hh_bf[:, kt, :], start=(kt == 0), stop=(kt == FHT - 1)),
                        inc="st")
                d_mm = k.now("st")
                k.wait("vector", "st", d_mm)
                k.op("vector", lambda h, bank=bank: h.tensor_scalar(
                    sg_scr[:, :], bank[:, :], -C127_3, 25.0,
                    ALU.mult, ALU.min), inc="sv")
                k.wait("scalar", "sv", k.now("sv"))
                k.op("scalar", lambda h: h.activation(
                    sg_scr[:, :], sg_scr[:, :], AF.Exp), inc="ss")
                k.op("scalar", lambda h: h.activation(
                    sg_scr[:, :], sg_scr[:, :], AF.Ln, bias=1.0), inc="ss")
                k.op("scalar", lambda h: h.activation(
                    sg_scr[:, :], sg_scr[:, :], AF.Exp, scale=-1.0), inc="ss")
                k.wait("vector", "ss", k.now("ss"))
                slot = do % 2
                k.wait_s("vector", "qgs%d" % slot, gev_ev[slot])
                k.op("vector", lambda h, slot=slot, bank=bank: h.tensor_mul(
                    g_ev[:, slot, :], sg_scr[:, :], bank[:, :]), inc="sv")
                gmul_ev[slot] = k.now("sv")
                k.wait("sync", "sv", k.now("sv"))
                k.dma("sync", g_sp[:, do, m0:m0 + 512], g_ev.ap()[:, slot, :],
                      "qgs%d" % slot)
                gev_ev[slot] = k.now("qgs%d" % slot)
            a2mm_prev = k.now("st")
        F.close()

        for snm in list(k.sems):
            if snm.startswith("q"):
                k.wait("sync", snm, k.now(snm))
        block = ctx.enter_context(nc.Block())
        k.emit(block)
    return nc


def build_launch2():
    nc = bass.Bass("TRN2", target_bir_lowering=False)
    x_b = nc.dram_tensor("x_b", [TOK, D], F32, kind="ExternalInput").ap()
    xn_sp = nc.dram_tensor("xn_sp", [128, CH, J], F32, kind="ExternalInput").ap()
    k_sp = nc.dram_tensor("k_sp", [128, CH, J], F32, kind="ExternalInput").ap()
    v_sp = nc.dram_tensor("v_sp", [128, CH, J], F32, kind="ExternalInput").ap()
    r_sp = nc.dram_tensor("r_sp", [128, CH, J], F32, kind="ExternalInput").ap()
    g_sp = nc.dram_tensor("g_sp", [128, CH, TOK], F32, kind="ExternalInput").ap()
    summ_all = nc.dram_tensor("summ_all", [128, NCORES, 3, CH], F32,
                              kind="ExternalInput").ap()
    wo = nc.dram_tensor("wo", [D, D], F32, kind="ExternalInput").ap()
    cwr = nc.dram_tensor("cwr", [D, D], F32, kind="ExternalInput").ap()
    cwk = nc.dram_tensor("cwk", [D, FF], F32, kind="ExternalInput").ap()
    cwv = nc.dram_tensor("cwv", [FF, D], F32, kind="ExternalInput").ap()
    vecs = {}
    for nm in ("ln2_g", "ln2_b", "cmix_k", "cmix_r", "tdec", "tfirst"):
        vecs[nm] = nc.dram_tensor(nm, [128, CH], F32, kind="ExternalInput").ap()
    hmask = nc.dram_tensor("hmask", [128, 1], F32, kind="ExternalInput").ap()
    epsin = nc.dram_tensor("epsin", [128, 1], F32, kind="ExternalInput").ap()
    cmask = nc.dram_tensor("cmask", [128, NCORES], F32, kind="ExternalInput").ap()
    ident = nc.dram_tensor("ident", [128, 128], F32, kind="ExternalInput").ap()
    out = nc.dram_tensor("out", [TOK, D], F32, kind="ExternalOutput").ap()
    xmid_sp = nc.dram_tensor("xmid_sp", [128, CH, TOK], F32, kind="Internal").ap()
    uck_sp = nc.dram_tensor("uck_sp", [128, CH, TOK], BF16, kind="Internal").ap()
    rr_sp = nc.dram_tensor("rr_sp", [128, CH, TOK], F32, kind="Internal").ap()
    a2rr_sp = nc.dram_tensor("a2rr_sp", [128, CH, TOK], F32, kind="Internal").ap()

    with ExitStack() as ctx:
        k = K(nc, ctx)
        for s in ("qs", "qg", "sv", "ss", "st", "sg"):
            k.sem(ctx, s)
        sb = lambda nm, shape, dt: ctx.enter_context(nc.sbuf_tensor(nm, shape, dt))
        psb = lambda nm, w: ctx.enter_context(nc.psum_tensor(nm, [128, w], F32))

        tabs = {}
        for nm, ap_ in vecs.items():
            tabs[nm] = sb(nm + "_t", [128, CH], F32)
            k.dma("sync", tabs[nm][:, :], ap_, "qt")
        negw = sb("negw", [128, CH], F32)
        w_t = sb("w_t", [128, CH], F32)
        U_t = sb("U_t", [128, CH], F32)
        hm_t = sb("hm_t", [128, 1], F32)
        eps_t = sb("eps_t", [128, 1], F32)
        cm_t = sb("cm_t", [128, NCORES], F32)
        id_t = sb("id_t", [128, 128], F32)
        k.dma("sync", hm_t[:, :], hmask, "qt")
        k.dma("sync", eps_t[:, :], epsin, "qt")
        k.dma("sync", cm_t[:, :], cmask, "qt")
        k.dma("sync", id_t[:, :], ident, "qt")
        s1 = sb("s1", [128, 1], F32)
        s2 = sb("s2", [128, 1], F32)
        st1 = sb("st1", [128, 1], F32)
        st2 = sb("st2", [128, 1], F32)
        st3 = sb("st3", [128, 1], F32)
        cPP = sb("cPP", [128, CH], F32)
        cAA = sb("cAA", [128, CH], F32)
        cBB = sb("cBB", [128, CH], F32)
        cH2 = sb("cH2", [128, CH], F32)
        ps_a = psb("ps_a", D)
        ps_b = psb("ps_b", D)
        ps_m = [psb(f"ps_m{i}", 512) for i in range(2)]
        ps_k = psb("ps_k", 512)

        k.wait_s("scalar", "qt", k.now("qt"))
        k.op("scalar", lambda h: h.activation(
            negw[:, :], tabs["tdec"][:, :], AF.Exp), inc="ss")
        k.op("scalar", lambda h: h.activation(
            U_t[:, :], tabs["tfirst"][:, :], AF.Exp), inc="ss")
        k.wait("vector", "ss", k.now("ss"))
        k.op("vector", lambda h: h.tensor_scalar(
            w_t[:, :], negw[:, :], -1.0, None, ALU.mult), inc="sv")

        G = Phase(nc)
        summ_b = G.sb("summ_b", [128, NCORES, 3, CH], F32)
        f1 = G.sb("f1", [128, CH], F32)
        f2 = G.sb("f2", [128, CH], F32)
        f3 = G.sb("f3", [128, CH], F32)
        f4 = G.sb("f4", [128, CH], F32)
        f5 = G.sb("f5", [128, CH], F32)
        k.dma("sync", summ_b[:, :, :, :], summ_all, "qt2")
        k.autofence = True
        k.op("vector", lambda h: h.memset(cAA[:, :], 0.0), inc="sv")
        k.op("vector", lambda h: h.memset(cBB[:, :], 0.0), inc="sv")
        k.op("vector", lambda h: h.memset(cPP[:, :], NEG), inc="sv")
        k.wait_s("vector", "qt2", k.now("qt2"))
        k.wait("vector", "ss", 2)
        for jc in range(NCORES - 1):
            k.op("vector", lambda h: h.scalar_tensor_tensor(
                f1[:, :], negw[:, :], -2048.0, cPP[:, :], ALU.mult, ALU.add),
                inc="sv")
            k.op("vector", lambda h, jc=jc: h.tensor_max(
                f2[:, :], f1[:, :], summ_b[:, jc, 2, :]), inc="sv")
            k.op("vector", lambda h: h.tensor_sub(
                f1[:, :], f1[:, :], f2[:, :]), inc="sv")
            k.op("vector", lambda h, jc=jc: h.tensor_sub(
                f3[:, :], summ_b[:, jc, 2, :], f2[:, :]), inc="sv")
            k.wait("scalar", "sv", k.now("sv"))
            k.op("scalar", lambda h: h.activation(f1[:, :], f1[:, :], AF.Exp),
                 inc="ss")
            k.op("scalar", lambda h: h.activation(f3[:, :], f3[:, :], AF.Exp),
                 inc="ss")
            k.wait("vector", "ss", k.now("ss"))
            for acc, idx in ((cAA, 0), (cBB, 1)):
                k.op("vector", lambda h, acc=acc: h.tensor_mul(
                    f4[:, :], acc[:, :], f1[:, :]), inc="sv")
                k.op("vector", lambda h, jc=jc, idx=idx: h.tensor_mul(
                    f5[:, :], summ_b[:, jc, idx, :], f3[:, :]), inc="sv")
                k.op("vector", lambda h: h.tensor_add(
                    f4[:, :], f4[:, :], f5[:, :]), inc="sv")
                k.op("vector", lambda h, acc=acc: h.tensor_sub(
                    f4[:, :], f4[:, :], acc[:, :]), inc="sv")
                k.op("vector", lambda h, acc=acc, jc=jc: h.scalar_tensor_tensor(
                    acc[:, :], f4[:, :], cm_t[:, jc:jc + 1], acc[:, :],
                    ALU.mult, ALU.add), inc="sv")
            k.op("vector", lambda h: h.tensor_sub(
                f4[:, :], f2[:, :], cPP[:, :]), inc="sv")
            k.op("vector", lambda h, jc=jc: h.scalar_tensor_tensor(
                cPP[:, :], f4[:, :], cm_t[:, jc:jc + 1], cPP[:, :],
                ALU.mult, ALU.add), inc="sv")
        k.autofence = False
        fold_sv = k.now("sv")
        fold_ss = k.now("ss")
        G.close()

        B = Phase(nc)
        wo_s = B.sb("wo_s", [128, CH, D], BF16)
        cwr_s = B.sb("cwr_s", [128, CH, D], BF16)
        k.wait("gpsimd", "sv", fold_sv)
        k.wait("gpsimd", "ss", fold_ss)
        k.dma("gpsimd", wo_s[:, :, :],
              wo.rearrange("(kk p) e -> p kk e", p=128), "qgw")
        k.dma("gpsimd", cwr_s[:, :, :],
              cwr.rearrange("(kk p) e -> p kk e", p=128), "qgw")
        b1_w_loaded = k.now("qgw")
        NB = 256
        bK = [B.sb(f"bK{i}", [128, CH, NB], F32) for i in range(2)]
        bV = [B.sb(f"bV{i}", [128, CH, NB], F32) for i in range(2)]
        bR = [B.sb(f"bR{i}", [128, CH, NB], F32) for i in range(2)]
        bXN = [B.sb(f"bXN{i}", [128, CH, NB], F32) for i in range(2)]
        bPP = B.sb("bPP", [128, CH, NB], F32)
        bQ = B.sb("bQ", [128, CH, NB], F32)
        bD = B.sb("bD", [128, CH, NB], F32)
        bAA = B.sb("bAA", [128, CH, NB], F32)
        bBB = B.sb("bBB", [128, CH, NB], F32)
        bRW = B.sb("bRW", [128, CH, NB], BF16)
        bH2 = B.sb("bH2", [128, CH, NB], F32)
        bUCK = B.sb("bUCK", [128, CH, NB], BF16)
        bUCR = B.sb("bUCR", [128, CH, NB], BF16)
        bRRP = B.sb("bRRP", [128, 2, NB], F32)
        tm_buf = B.sb("tm_buf", [128, D], F32)
        tm_scr = B.sb("tm_scr", [128, D], F32)
        k.wait("sync", "sv", fold_sv)
        k.wait("sync", "ss", fold_ss)

        tiles = [(NB * t0, NB) for t0 in range(TOK // NB)] + [(TOK, 1)]
        ev_bK = {0: 0, 1: 0}
        ev_bV = {0: 0, 1: 0}
        ev_bR = {0: 0, 1: 0}
        ev_bXN_qs = {0: 0, 1: 0}
        ev_bXN_st = {0: 0, 1: 0}
        xm_ev = {0: 0, 1: 0}
        ev_womm = 0
        ev_uckspill = 0
        ev_cwrmm = 0
        ev_ln2ln = 0
        ev_h2tp = 0
        ev_h2evac = 0
        ev_tsclamp = 0
        rr_ev = {0: 0, 1: 0}
        first_mm = True
        for ti, (j0, n) in enumerate(tiles):
            p = ti % 2
            k.autofence = (n < 128)
            slow = {"allow_slow_non_contiguous": True} if n == 1 else {}
            qldp = "qsa" if p == 0 else "qsb"
            k.wait("sync", "sv", ev_bK[p])
            k.dma("sync", bK[p][:, :, 0:n], k_sp[:, :, j0:j0 + n], qldp, **slow)
            k.wait("sync", "sv", ev_bV[p])
            k.dma("sync", bV[p][:, :, 0:n], v_sp[:, :, j0:j0 + n], qldp, **slow)
            k.wait("sync", "sv", ev_bR[p])
            k.dma("sync", bR[p][:, :, 0:n], r_sp[:, :, j0:j0 + n], qldp, **slow)
            k.wait_s("sync", "qm%d" % p, ev_bXN_qs[p])
            k.wait("sync", "st", ev_bXN_st[p])
            k.dma("sync", bXN[p][:, :, 0:n], xn_sp[:, :, j0:j0 + n], qldp,
                  **slow)
            loads = k.now(qldp)
            k.wait_s("vector", qldp, loads)
            for c in range(CH):
                k.op("vector", lambda h, c=c, n=n, p=p: h.tensor_tensor_scan(
                    bPP[:, c, 0:n],
                    w_t[:, c:c + 1].broadcast_to([128, n]),
                    bK[p][:, c, 0:n], cPP[:, c:c + 1], ALU.add, ALU.max),
                    inc="sv")
            with k.af():
                k.op("vector", lambda h, p=p: h.tensor_sub(
                    bQ[:, :, 0:1], cPP.ap().unsqueeze(2), bPP[:, :, 0:1]),
                    inc="sv")
            if n > 1:
                k.op("vector", lambda h, n=n: h.tensor_sub(
                    bQ[:, :, 1:n], bPP[:, :, 0:n - 1], bPP[:, :, 1:n]),
                    inc="sv")
            k.op("vector", lambda h, n=n, p=p: h.tensor_sub(
                bK[p][:, :, 0:n], bK[p][:, :, 0:n], bPP[:, :, 0:n]), inc="sv")
            k.wait("scalar", "sv", k.now("sv"))
            for c in range(CH):
                k.op("scalar", lambda h, c=c, n=n: h.activation(
                    bD[:, c, 0:n], bQ[:, c, 0:n], AF.Exp,
                    bias=w_t[:, c:c + 1]), inc="ss")
            k.op("scalar", lambda h, n=n: h.activation(
                bQ[:, :, 0:n], bQ[:, :, 0:n], AF.Exp), inc="ss")
            k.op("scalar", lambda h, n=n, p=p: h.activation(
                bK[p][:, :, 0:n], bK[p][:, :, 0:n], AF.Exp), inc="ss")
            k.wait("vector", "ss", k.now("ss"))
            k.op("vector", lambda h, n=n, p=p: h.tensor_mul(
                bV[p][:, :, 0:n], bK[p][:, :, 0:n], bV[p][:, :, 0:n]),
                inc="sv")
            for c in range(CH):
                k.op("vector", lambda h, c=c, n=n, p=p: h.tensor_tensor_scan(
                    bAA[:, c, 0:n], bD[:, c, 0:n], bV[p][:, c, 0:n],
                    cAA[:, c:c + 1], ALU.mult, ALU.add), inc="sv")
                k.op("vector", lambda h, c=c, n=n, p=p: h.tensor_tensor_scan(
                    bBB[:, c, 0:n], bD[:, c, 0:n], bK[p][:, c, 0:n],
                    cBB[:, c:c + 1], ALU.mult, ALU.add), inc="sv")
            with k.af():
                k.op("vector", lambda h: h.tensor_mul(
                    bD[:, :, 0:1], bQ[:, :, 0:1], cAA.ap().unsqueeze(2)),
                    inc="sv")
            if n > 1:
                k.op("vector", lambda h, n=n: h.tensor_mul(
                    bD[:, :, 1:n], bQ[:, :, 1:n], bAA[:, :, 0:n - 1]), inc="sv")
            for c in range(CH):
                k.op("vector", lambda h, c=c, n=n, p=p: h.scalar_tensor_tensor(
                    bD[:, c, 0:n], bV[p][:, c, 0:n], U_t[:, c:c + 1],
                    bD[:, c, 0:n], ALU.mult, ALU.add), inc="sv")
            with k.af():
                k.op("vector", lambda h, p=p: h.tensor_mul(
                    bV[p][:, :, 0:1], bQ[:, :, 0:1], cBB.ap().unsqueeze(2)),
                    inc="sv")
            if n > 1:
                k.op("vector", lambda h, n=n, p=p: h.tensor_mul(
                    bV[p][:, :, 1:n], bQ[:, :, 1:n], bBB[:, :, 0:n - 1]),
                    inc="sv")
            for c in range(CH):
                k.op("vector", lambda h, c=c, n=n, p=p: h.scalar_tensor_tensor(
                    bV[p][:, c, 0:n], bK[p][:, c, 0:n], U_t[:, c:c + 1],
                    bV[p][:, c, 0:n], ALU.mult, ALU.add), inc="sv")
            with k.af():
                k.op("vector", lambda h, n=n: h.tensor_copy(
                    cPP[:, :], bPP.ap()[:, :, n - 1]), inc="sv")
                k.op("vector", lambda h, n=n: h.tensor_copy(
                    cAA[:, :], bAA.ap()[:, :, n - 1]), inc="sv")
                k.op("vector", lambda h, n=n: h.tensor_copy(
                    cBB[:, :], bBB.ap()[:, :, n - 1]), inc="sv")
            k.op("vector", lambda h, n=n, p=p: h.tensor_scalar(
                bR[p][:, :, 0:n], bR[p][:, :, 0:n], -1.0, 20.0, ALU.mult,
                ALU.min), inc="sv")
            k.wait("scalar", "sv", k.now("sv"))
            k.op("scalar", lambda h, n=n, p=p: h.activation(
                bR[p][:, :, 0:n], bR[p][:, :, 0:n], AF.Exp), inc="ss")
            k.wait("vector", "ss", k.now("ss"))
            k.op("vector", lambda h, n=n, p=p: h.tensor_mul(
                bK[p][:, :, 0:n], bV[p][:, :, 0:n], bR[p][:, :, 0:n]),
                inc="sv")
            ev_bR[p] = k.now("sv")
            k.op("vector", lambda h, n=n, p=p: h.tensor_add(
                bV[p][:, :, 0:n], bV[p][:, :, 0:n], bK[p][:, :, 0:n]),
                inc="sv")
            ev_bK[p] = k.now("sv")
            k.wait("scalar", "sv", k.now("sv"))
            k.op("scalar", lambda h, n=n, p=p: h.activation(
                bV[p][:, :, 0:n], bV[p][:, :, 0:n], AF.Ln), inc="ss")
            k.op("scalar", lambda h, n=n, p=p: h.activation(
                bV[p][:, :, 0:n], bV[p][:, :, 0:n], AF.Exp, scale=-1.0),
                inc="ss")
            k.wait("vector", "ss", k.now("ss"))
            k.wait("vector", "st", ev_womm)
            k.op("vector", lambda h, n=n, p=p: h.tensor_mul(
                bRW[:, :, 0:n], bD[:, :, 0:n], bV[p][:, :, 0:n]), inc="sv")
            ev_bV[p] = k.now("sv")
            rw_done = k.now("sv")
            k.wait("tensor", "sv", rw_done)
            if first_mm:
                k.wait_s("tensor", "qgw", b1_w_loaded)
                first_mm = False
            for o in range(CH):
                bank = ps_m[o % 2]
                k.wait("tensor", "sv", xm_ev[o % 2])
                for kt in range(CH):
                    k.op("tensor", lambda h, o=o, kt=kt, n=n, bank=bank:
                         h.matmul(bank[:, 0:n],
                                  wo_s[:, kt, 128 * o:128 * (o + 1)],
                                  bRW[:, kt, 0:n], start=(kt == 0),
                                  stop=(kt == 7)), inc="st")
                k.wait("vector", "st", k.now("st"))
                k.op("vector", lambda h, o=o, n=n, p=p, bank=bank: h.tensor_add(
                    bXN[p][:, o, 0:n], bXN[p][:, o, 0:n], bank[:, 0:n]),
                    inc="sv")
                xm_ev[o % 2] = k.now("sv")
            ev_womm = k.now("st")
            xmid_done = k.now("sv")
            u0 = max(1, j0)
            cnt = j0 + n - u0
            off = u0 - j0
            slowc = {"allow_slow_non_contiguous": True} if cnt == 1 else {}
            k.wait("sync", "sv", xmid_done)
            k.dma("sync", xmid_sp[:, :, u0 - 1:u0 - 1 + cnt],
                  bXN[p].ap()[:, :, off:off + cnt], "qm%d" % p, **slowc)
            nblk = (n + 127) // 128
            for tb in range(nblk):
                w = min(128, n - 128 * tb)
                k.wait("tensor", "sv", xmid_done)
                k.wait("tensor", "ss", ev_ln2ln)
                for c in range(CH):
                    k.op("tensor", lambda h, c=c, tb=tb, w=w, p=p: h.transpose(
                        ps_a[:w, 128 * c:128 * (c + 1)],
                        bXN[p][:, c, 128 * tb:128 * tb + w], id_t[:, :]),
                        inc="st")
                tp_done = k.now("st")
                ev_bXN_st[p] = tp_done
                k.wait("scalar", "st", max(tp_done, ev_h2tp))
                ev_ln2ln = ln_tokmajor(k, ps_a[:w, :], tm_buf[:w, :], w,
                                       tm_scr, s1, s2, st1, st2, st3, eps_t)
                k.wait("tensor", "ss", ev_ln2ln)
                k.wait("tensor", "sv", ev_h2evac)
                for c in range(CH):
                    k.op("tensor", lambda h, c=c, w=w: h.transpose(
                        ps_b[:, 128 * c:128 * c + w],
                        tm_buf[:w, 128 * c:128 * (c + 1)], id_t[:w, :w]),
                        inc="st")
                ev_h2tp = k.now("st")
                k.wait("vector", "st", ev_h2tp)
                for c in range(CH):
                    k.op("vector", lambda h, c=c, tb=tb, w=w: h.tensor_scalar(
                        bH2[:, c, 128 * tb:128 * tb + w],
                        ps_b[:, 128 * c:128 * c + w],
                        tabs["ln2_g"][:, c:c + 1], tabs["ln2_b"][:, c:c + 1],
                        ALU.mult, ALU.add), inc="sv")
                ev_h2evac = k.now("sv")
            if ti == 0:
                with k.af():
                    k.op("vector", lambda h: h.tensor_scalar(
                        bH2[:, :, 0:1], bH2[:, :, 0:1], hm_t[:, 0:1], None,
                        ALU.mult), inc="sv")
            for mtn, ub, war_ev in (("cmix_k", bUCK, ("qu", ev_uckspill)),
                                    ("cmix_r", bUCR, ("st", ev_cwrmm))):
                mt = tabs[mtn]
                k.wait_s("vector", war_ev[0], war_ev[1])
                for c in range(CH):
                    if off == 0:
                        with k.af():
                            k.op("vector", lambda h, c=c: (
                                h.tensor_sub(bD[:, c, 0:1], bH2[:, c, 0:1],
                                             cH2.ap()[:, c:c + 1].unsqueeze(2))),
                                inc="sv")
                            k.op("vector", lambda h, c=c, mt=mt, ub=ub:
                                 h.scalar_tensor_tensor(
                                     ub[:, c, 0:1], bD[:, c, 0:1], mt[:, c:c + 1],
                                     cH2.ap()[:, c:c + 1].unsqueeze(2),
                                     ALU.mult, ALU.add), inc="sv")
                        if cnt > 1:
                            k.op("vector", lambda h, c=c, cnt=cnt: h.tensor_sub(
                                bD[:, c, 1:cnt], bH2[:, c, 1:cnt],
                                bH2[:, c, 0:cnt - 1]), inc="sv")
                            k.op("vector", lambda h, c=c, mt=mt, ub=ub, cnt=cnt:
                                 h.scalar_tensor_tensor(
                                     ub[:, c, 1:cnt], bD[:, c, 1:cnt],
                                     mt[:, c:c + 1], bH2[:, c, 0:cnt - 1],
                                     ALU.mult, ALU.add), inc="sv")
                    else:
                        k.op("vector", lambda h, c=c, cnt=cnt, off=off:
                             h.tensor_sub(
                                 bD[:, c, 0:cnt], bH2[:, c, off:off + cnt],
                                 bH2[:, c, off - 1:off - 1 + cnt]), inc="sv")
                        k.op("vector", lambda h, c=c, mt=mt, ub=ub, cnt=cnt,
                             off=off: h.scalar_tensor_tensor(
                                 ub[:, c, 0:cnt], bD[:, c, 0:cnt],
                                 mt[:, c:c + 1],
                                 bH2[:, c, off - 1:off - 1 + cnt],
                                 ALU.mult, ALU.add), inc="sv")
            mix_done = k.now("sv")
            with k.af():
                k.op("vector", lambda h, n=n: h.tensor_copy(
                    cH2[:, :], bH2.ap()[:, :, n - 1]), inc="sv")
            k.wait("sync", "sv", mix_done)
            k.dma("sync", uck_sp[:, :, u0 - 1:u0 - 1 + cnt],
                  bUCK.ap()[:, :, 0:cnt], "qu", **slowc)
            ev_uckspill = k.now("qu")
            k.wait("tensor", "sv", mix_done)
            for o in range(CH):
                k.wait("tensor", "sv", ev_tsclamp)
                for kt in range(CH):
                    k.op("tensor", lambda h, o=o, kt=kt, cnt=cnt: h.matmul(
                        ps_k[:, 0:cnt], cwr_s[:, kt, 128 * o:128 * (o + 1)],
                        bUCR[:, kt, 0:cnt], start=(kt == 0), stop=(kt == 7)),
                        inc="st")
                k.wait("vector", "st", k.now("st"))
                k.wait("vector", "ss", k.now("ss"))
                k.op("vector", lambda h, cnt=cnt: h.tensor_scalar(
                    tm_scr[:, 0:cnt], ps_k[:, 0:cnt], -1.0, 20.0,
                    ALU.mult, ALU.min), inc="sv")
                ev_tsclamp = k.now("sv")
                k.wait("scalar", "sv", ev_tsclamp)
                slot = o % 2
                k.wait_s("scalar", "qrr%d" % slot, rr_ev[slot])
                k.op("scalar", lambda h, cnt=cnt: h.activation(
                    tm_scr[:, 0:cnt], tm_scr[:, 0:cnt], AF.Exp), inc="ss")
                k.op("scalar", lambda h, cnt=cnt: h.activation(
                    tm_scr[:, 0:cnt], tm_scr[:, 0:cnt], AF.Ln, bias=1.0),
                    inc="ss")
                k.op("scalar", lambda h, cnt=cnt, slot=slot: h.activation(
                    bRRP[:, slot, 0:cnt], tm_scr[:, 0:cnt], AF.Exp,
                    scale=-1.0), inc="ss")
                k.wait("sync", "ss", k.now("ss"))
                k.dma("sync", rr_sp[:, o, u0 - 1:u0 - 1 + cnt],
                      bRRP.ap()[:, slot, 0:cnt], "qrr%d" % slot, **slowc)
                rr_ev[slot] = k.now("qrr%d" % slot)
            ev_cwrmm = k.now("st")
            ev_bXN_qs[p] = k.now("qm%d" % p)
            k.autofence = False
        B.close()

        B2 = Phase(nc)
        cwk_s = B2.sb("cwk_s", [128, CH, FF], BF16)
        cwv_s = B2.sb("cwv_s", [128, FFT, D], BF16)
        k.wait("gpsimd", "sv", k.now("sv"))
        k.wait("gpsimd", "ss", k.now("ss"))
        k.wait("gpsimd", "st", k.now("st"))
        for snm in ("qsa", "qsb", "qm0", "qm1", "qu", "qrr0", "qrr1"):
            k.wait_s("gpsimd", snm, k.now(snm))
        k.dma("gpsimd", cwk_s[:, :, :],
              cwk.rearrange("(kk p) f -> p kk f", p=128), "qgw")
        k.dma("gpsimd", cwv_s[:, :, :],
              cwv.rearrange("(kk p) e -> p kk e", p=128), "qgw")
        b2_w_loaded = k.now("qgw")
        k.wait_s("tensor", "qgw", b2_w_loaded)
        bUK2 = [B2.sb(f"bUK2{i}", [128, CH, 512], BF16) for i in range(2)]
        _bRR2 = B2.sb("bRR2", [128, CH, 512], F32)
        bRR2 = [_bRR2, _bRR2]
        kk_bf = B2.sb("kk_bf", [128, FFT, 512], BF16)
        scr2 = B2.sb("scr2", [128, 512], F32)
        a2ev = B2.sb("a2ev", [128, 2, 512], F32)
        kk_banks = [ps_k.ap()[:, :], ps_b.ap()[:, 0:512]]
        relu_ev = {0: 0, 1: 0}
        a2_ev = {0: 0, 1: 0}
        a2tt_ev = {0: 0, 1: 0}
        ev_kkmm = {0: 0, 1: 0}
        ev_cvvmm = 0
        k.wait("tensor", "sv", k.now("sv"))
        for b2 in range(4):
            m0 = 512 * b2
            p = b2 % 2
            qb = "qb%d" % p
            k.wait("sync", "st", ev_kkmm[p])
            k.dma("sync", bUK2[p][:, :, :], uck_sp[:, :, m0:m0 + 512], qb)
            k.wait("sync", "sv", max(a2tt_ev[0], a2tt_ev[1]))
            k.dma("sync", bRR2[p][:, :, :], rr_sp[:, :, m0:m0 + 512], "qbr")
            k.wait_s("tensor", qb, k.now(qb))
            k.wait_s("vector", "qbr", k.now("qbr"))
            k.wait("scalar", "st", ev_cvvmm)
            for fo in range(FFT):
                bank = kk_banks[fo % 2]
                k.wait("tensor", "sv", relu_ev[fo % 2])
                for kt in range(CH):
                    k.op("tensor", lambda h, fo=fo, kt=kt, p=p, bank=bank:
                         h.matmul(bank, cwk_s[:, kt, 128 * fo:128 * (fo + 1)],
                                  bUK2[p][:, kt, :], start=(kt == 0),
                                  stop=(kt == 7)), inc="st")
                k.wait("vector", "st", k.now("st"))
                k.wait("vector", "ss", k.now("ss"))
                k.op("vector", lambda h, bank=bank: h.tensor_scalar(
                    scr2[:, :], bank, 0.0, None, ALU.max), inc="sv")
                relu_ev[fo % 2] = k.now("sv")
                k.wait("scalar", "sv", k.now("sv"))
                k.op("scalar", lambda h, fo=fo: h.activation(
                    kk_bf[:, fo, :], scr2[:, :], AF.Square), inc="ss")
            ev_kkmm[p] = k.now("st")
            kk_done = k.now("ss")
            k.wait("tensor", "ss", kk_done)
            for o in range(CH):
                bank = ps_m[o % 2]
                k.wait("tensor", "sv", a2tt_ev[o % 2])
                for kt in range(FFT):
                    k.op("tensor", lambda h, o=o, kt=kt, bank=bank: h.matmul(
                        bank[:, :], cwv_s[:, kt, 128 * o:128 * (o + 1)],
                        kk_bf[:, kt, :], start=(kt == 0), stop=(kt == FFT - 1)),
                        inc="st")
                k.wait("vector", "st", k.now("st"))
                slot = o % 2
                k.wait_s("vector", "qa%d" % slot, a2_ev[slot])
                k.op("vector", lambda h, o=o, p=p, slot=slot, bank=bank:
                     h.tensor_mul(a2ev[:, slot, :], bRR2[p][:, o, :],
                                  bank[:, :]), inc="sv")
                a2tt_ev[slot] = k.now("sv")
                k.wait("sync", "sv", k.now("sv"))
                k.dma("sync", a2rr_sp[:, o, m0:m0 + 512],
                      a2ev.ap()[:, slot, :], "qa%d" % slot)
                a2_ev[slot] = k.now("qa%d" % slot)
            ev_cvvmm = k.now("st")
        B2.close()

        B3 = Phase(nc)
        bXM3 = [B3.sb(f"bXM3{i}", [128, CH, 512], F32) for i in range(2)]
        bA23 = [B3.sb(f"bA23{i}", [128, CH, 512], F32) for i in range(2)]
        bG3 = [B3.sb(f"bG3{i}", [128, CH, 512], F32) for i in range(2)]
        x_tm3 = [B3.sb(f"x_tm3{i}", [128, 4, D], F32) for i in range(2)]
        o_tm = B3.sb("o_tm", [128, 2, D], F32)
        k.wait("sync", "sv", k.now("sv"))
        k.wait("sync", "ss", k.now("ss"))
        k.wait("sync", "st", k.now("st"))
        ev_b3v = {0: 0, 1: 0}
        ev_b3tp = {0: 0, 1: 0}
        ev_otm = {0: 0, 1: 0}
        tp3_ev = 0
        ev_b3stt = {0: 0, 1: 0}
        ev_sttlast = 0
        for b3 in range(4):
            m0 = 512 * b3
            p = b3 % 2
            qc = "qc%d" % p
            k.wait("sync", "sv", ev_b3stt[p])
            k.wait("sync", "st", ev_b3tp[p])
            k.dma("sync", bXM3[p][:, :, :], xmid_sp[:, :, m0:m0 + 512], qc)
            k.dma("sync", bA23[p][:, :, :], a2rr_sp[:, :, m0:m0 + 512], qc)
            k.dma("sync", bG3[p][:, :, :], g_sp[:, :, m0:m0 + 512], qc)
            for b in range(4):
                k.dma("sync", x_tm3[p][:, b, :],
                      x_b[m0 + 128 * b:m0 + 128 * (b + 1), :], qc)
            k.wait_s("vector", qc, k.now(qc))
            k.op("vector", lambda h, p=p: h.tensor_add(
                bXM3[p][:, :, :], bXM3[p][:, :, :], bA23[p][:, :, :]),
                inc="sv")
            k.op("vector", lambda h, p=p: h.tensor_mul(
                bXM3[p][:, :, :], bXM3[p][:, :, :], bG3[p][:, :, :]),
                inc="sv")
            sdone = k.now("sv")
            ev_b3v[p] = sdone
            k.wait("tensor", "sv", sdone)
            for tb in range(4):
                ot_slot = tb % 2
                k.wait("tensor", "sv", ev_sttlast)
                for c in range(CH):
                    k.op("tensor", lambda h, c=c, tb=tb, p=p: h.transpose(
                        ps_a[:, 128 * c:128 * (c + 1)],
                        bXM3[p][:, c, 128 * tb:128 * (tb + 1)], id_t[:, :]),
                        inc="st")
                tp3_ev = k.now("st")
                k.wait("vector", "st", tp3_ev)
                k.wait_s("vector", "qo%d" % ot_slot, ev_otm[ot_slot])
                k.op("vector", lambda h, tb=tb, p=p, ot=ot_slot:
                     h.scalar_tensor_tensor(
                         o_tm[:, ot, :], ps_a[:, :], C127_3,
                         x_tm3[p][:, tb, :], ALU.mult, ALU.add), inc="sv")
                odone = k.now("sv")
                ev_sttlast = odone
                k.wait("sync", "sv", odone)
                k.dma("sync", out[m0 + 128 * tb:m0 + 128 * (tb + 1), :],
                      o_tm.ap()[:, ot_slot, :], "qo%d" % ot_slot)
                ev_otm[ot_slot] = k.now("qo%d" % ot_slot)
                ev_b3stt[p] = odone
            ev_b3tp[p] = tp3_ev
        B3.close()

        for snm in list(k.sems):
            if snm.startswith("q"):
                k.wait("sync", snm, k.now(snm))
        block = ctx.enter_context(nc.Block())
        k.emit(block)
    return nc


def _vec_layout(v):
    return np.ascontiguousarray(np.asarray(v, np.float32).reshape(CH, 128).T)


def kernel(L0_attn_x, L0_attn_ln1_g, L0_attn_ln1_b, L0_attn_ln2_g,
           L0_attn_ln2_b, L0_attn_tm_mix_k, L0_attn_tm_mix_v, L0_attn_tm_mix_r,
           L0_attn_time_decay, L0_attn_time_first, L0_attn_wk, L0_attn_wv,
           L0_attn_wr, L0_attn_wo, L0_attn_cm_mix_k, L0_attn_cm_mix_r,
           L0_attn_cwk, L0_attn_cwv, L0_attn_cwr, L0_ffn_w_gate, L0_ffn_w_up,
           L0_ffn_w_down):
    x = np.ascontiguousarray(np.asarray(L0_attn_x, np.float32).reshape(T, D))
    f32 = lambda a: np.ascontiguousarray(np.asarray(a, np.float32))
    i32 = lambda a: np.ascontiguousarray(np.asarray(a, np.int32))
    ident = np.eye(128, dtype=np.float32)
    ramp = np.ascontiguousarray(np.tile(np.arange(128, dtype=np.float32),
                                        (128, 1)))
    epsv = np.full((128, 1), EPS, np.float32)

    common1 = {
        "wk": f32(L0_attn_wk), "wv": f32(L0_attn_wv), "wr": f32(L0_attn_wr),
        "wg_i": i32(L0_ffn_w_gate), "wu_i": i32(L0_ffn_w_up),
        "wd_i": i32(L0_ffn_w_down),
        "ln1_g": _vec_layout(L0_attn_ln1_g), "ln1_b": _vec_layout(L0_attn_ln1_b),
        "mix_k": _vec_layout(L0_attn_tm_mix_k),
        "mix_v": _vec_layout(L0_attn_tm_mix_v),
        "mix_r": _vec_layout(L0_attn_tm_mix_r),
        "tdec": _vec_layout(L0_attn_time_decay),
        "ident": ident, "ramp": ramp, "epsin": epsv,
    }
    maps1 = []
    for i in range(NCORES):
        s = TOK * i
        xa = np.zeros((TOK + 2, D), np.float32)
        if i == 0:
            xa[2:] = x[0:TOK]
        else:
            xa[:] = x[s - 2:s + TOK]
        hm = np.zeros((128, 1), np.float32) if i == 0 else \
            np.ones((128, 1), np.float32)
        kb = np.full((128, 1), -1.0e30, np.float32) if i == 0 else \
            np.zeros((128, 1), np.float32)
        maps1.append({**common1, "x_a1": xa, "hmask": hm, "kbias": kb})

    nc1 = build_launch1()
    res1 = run_bass_kernel_spmd(nc1, maps1, core_ids=list(range(NCORES)))

    summ_all = np.stack([np.asarray(res1.results[i]["summ"])
                         for i in range(NCORES)], axis=0)
    summ_all = np.ascontiguousarray(summ_all.transpose(2, 0, 1, 3))

    common2 = {
        "wo": f32(L0_attn_wo), "cwr": f32(L0_attn_cwr),
        "cwk": f32(L0_attn_cwk), "cwv": f32(L0_attn_cwv),
        "ln2_g": _vec_layout(L0_attn_ln2_g), "ln2_b": _vec_layout(L0_attn_ln2_b),
        "cmix_k": _vec_layout(L0_attn_cm_mix_k),
        "cmix_r": _vec_layout(L0_attn_cm_mix_r),
        "tdec": _vec_layout(L0_attn_time_decay),
        "tfirst": _vec_layout(L0_attn_time_first),
        "ident": ident, "summ_all": summ_all, "epsin": epsv,
    }
    maps2 = []
    for i in range(NCORES):
        s = TOK * i
        hm = np.zeros((128, 1), np.float32) if i == 0 else \
            np.ones((128, 1), np.float32)
        cm = np.zeros((128, NCORES), np.float32)
        cm[:, :i] = 1.0
        r1 = res1.results[i]
        maps2.append({**common2, "x_b": np.ascontiguousarray(x[s:s + TOK]),
                      "hmask": hm, "cmask": cm,
                      "xn_sp": np.asarray(r1["xn_sp"]),
                      "k_sp": np.asarray(r1["k_sp"]),
                      "v_sp": np.asarray(r1["v_sp"]),
                      "r_sp": np.asarray(r1["r_sp"]),
                      "g_sp": np.asarray(r1["g_sp"])})

    nc2 = build_launch2()
    res2 = run_bass_kernel_spmd(nc2, maps2, core_ids=list(range(NCORES)))
    out = np.concatenate([np.asarray(res2.results[i]["out"])
                          for i in range(NCORES)], axis=0)
    return out.reshape(1, T, D)


# version 04
# speedup vs baseline: 1.2074x; 1.0111x over previous
import numpy as np
import concourse.bass as bass
import concourse.mybir as mybir
from concourse.bass_utils import run_bass_kernel_spmd
from contextlib import ExitStack

F32 = mybir.dt.float32
BF16 = mybir.dt.bfloat16
I32 = mybir.dt.int32
AF = mybir.ActivationFunctionType
ALU = mybir.AluOpType
AX = mybir.AxisListType.X

NCORES = 8
T = 16384
D = 1024
CH = 8
FF = 4096
FH = 3072
FHT = FH // 128
FFT = FF // 128
TOK = T // NCORES
J = TOK + 1
NEG = -1.0e38
EPS = 1e-5
C127_3 = float(1.0 / (127.0 ** 3))


class K:
    ENGINES = ("sync", "gpsimd", "vector", "scalar", "tensor")

    PROG = {"vector": "sv", "scalar": "ss", "gpsimd": "sg", "tensor": "st"}

    def __init__(self, nc, ctx=None):
        self.nc = nc
        self.ctx = ctx
        self.streams = {e: [] for e in self.ENGINES}
        self.led = {}
        self.sems = {}
        self.autofence = False

    def sem(self, ctx, name):
        self.sems[name] = ctx.enter_context(self.nc.semaphore(name))
        self.led[name] = 0

    def ensure(self, name):
        if name not in self.sems:
            self.sem(self.ctx, name)
        return name

    def op(self, eng, fn, inc=None, n=1):
        fence = self.autofence and eng in ("vector", "scalar")
        if fence and inc is None:
            inc = self.PROG[eng]
        if inc is not None:
            self.led[inc] += n
        self.streams[eng].append(("i", fn, inc, n))
        if fence:
            self.wait(eng, inc, self.led[inc])
        return self.led[inc] if inc else None

    from contextlib import contextmanager as _cm

    @_cm
    def af(self):
        prev = self.autofence
        self.autofence = True
        try:
            yield
        finally:
            self.autofence = prev

    def dma(self, eng, out, in_, inc="qs", **kw):
        self.ensure(inc)
        fn = (lambda out=out, in_=in_, kw=kw:
              (lambda h: h.dma_start(out=out, in_=in_, **kw)))()
        return self.op(eng, fn, inc=inc, n=16)

    def wait_s(self, eng, sem, val):
        self.ensure(sem)
        self.wait(eng, sem, val)

    def wait(self, eng, sem, val):
        if val > 0:
            self.streams[eng].append(("w", sem, val))

    def now(self, sem):
        self.ensure(sem)
        return self.led[sem]

    def tile_barrier(self, sems=("qs", "sv", "ss", "st")):
        counts = [(s, self.led[s]) for s in sems]
        for eng in self.ENGINES:
            for s, v in counts:
                self.wait(eng, s, v)

    def emit(self, block):
        def mk(eng_name):
            stream = self.streams[eng_name]
            sems = self.sems

            def body(h):
                for item in stream:
                    if item[0] == "i":
                        _, fn, inc, n = item
                        ins = fn(h)
                        if inc is not None:
                            ins.then_inc(sems[inc], n)
                    else:
                        _, sem, val = item
                        h.wait_ge(sems[sem], val)
            return body
        block.sync(mk("sync"))
        block.gpsimd(mk("gpsimd"))
        block.vector(mk("vector"))
        block.scalar(mk("scalar"))
        block.tensor(mk("tensor"))


class Phase:

    def __init__(self, nc):
        self.nc = nc
        self.stack = ExitStack()

    def sb(self, name, shape, dt):
        return self.stack.enter_context(self.nc.sbuf_tensor(name, shape, dt))

    def close(self):
        self.stack.close()


def ln_tokmajor(k, blk_in, blk_out, npart, scr, s1, s2, st1, st2, st3,
                epsv):
    with k.af():
        k.op("scalar", lambda h: h.activation(
            scr[:npart, :], blk_in, AF.Square, accum_out=s2[:npart, :]),
            inc="ss")
        k.op("scalar", lambda h: h.activation(
            scr[:npart, :], blk_in, AF.Identity, accum_out=s1[:npart, :]),
            inc="ss")
        k.op("scalar", lambda h: h.mul(st1[:npart, :], s1[:npart, :], 1.0 / D),
             inc="ss")
        k.op("scalar", lambda h: h.activation(
            st2[:npart, :], st1[:npart, :], AF.Square), inc="ss")
        k.op("scalar", lambda h: h.activation(
            st2[:npart, :], st2[:npart, :], AF.Identity, scale=-1.0,
            bias=epsv[:npart, :]), inc="ss")
        k.op("scalar", lambda h: h.activation(
            st2[:npart, :], s2[:npart, :], AF.Identity, scale=1.0 / D,
            bias=st2[:npart, 0:1]), inc="ss")
        k.op("scalar", lambda h: h.activation(
            st2[:npart, :], st2[:npart, :], AF.Ln), inc="ss")
        k.op("scalar", lambda h: h.activation(
            st2[:npart, :], st2[:npart, :], AF.Exp, scale=-0.5), inc="ss")
        k.op("scalar", lambda h: h.mul(st3[:npart, :], st1[:npart, :],
                                       st2[:npart, 0:1]), inc="ss")
        k.op("scalar", lambda h: h.mul(st3[:npart, :], st3[:npart, :], -1.0),
             inc="ss")
        k.op("scalar", lambda h: h.activation(
            blk_out, blk_in, AF.Identity, bias=st3[:npart, 0:1],
            scale=st2[:npart, 0:1]), inc="ss")
    return k.now("ss")


def build_launch1():
    nc = bass.Bass("TRN2", target_bir_lowering=False)
    x_a1 = nc.dram_tensor("x_a1", [TOK + 2, D], F32, kind="ExternalInput").ap()
    wk = nc.dram_tensor("wk", [D, D], F32, kind="ExternalInput").ap()
    wv = nc.dram_tensor("wv", [D, D], F32, kind="ExternalInput").ap()
    wr = nc.dram_tensor("wr", [D, D], F32, kind="ExternalInput").ap()
    wg_i = nc.dram_tensor("wg_i", [FH, D], I32, kind="ExternalInput").ap()
    wu_i = nc.dram_tensor("wu_i", [FH, D], I32, kind="ExternalInput").ap()
    wd_i = nc.dram_tensor("wd_i", [D, FH], I32, kind="ExternalInput").ap()
    vecs = {}
    for nm in ("ln1_g", "ln1_b", "mix_k", "mix_v", "mix_r", "tdec"):
        vecs[nm] = nc.dram_tensor(nm, [128, CH], F32, kind="ExternalInput").ap()
    hmask = nc.dram_tensor("hmask", [128, 1], F32, kind="ExternalInput").ap()
    kbias = nc.dram_tensor("kbias", [128, 1], F32, kind="ExternalInput").ap()
    epsin = nc.dram_tensor("epsin", [128, 1], F32, kind="ExternalInput").ap()
    ident = nc.dram_tensor("ident", [128, 128], F32, kind="ExternalInput").ap()
    ramp = nc.dram_tensor("ramp", [128, 128], F32, kind="ExternalInput").ap()

    xn_sp = nc.dram_tensor("xn_sp", [128, CH, J], F32, kind="ExternalOutput").ap()
    k_sp = nc.dram_tensor("k_sp", [128, CH, J], F32, kind="ExternalOutput").ap()
    v_sp = nc.dram_tensor("v_sp", [128, CH, J], F32, kind="ExternalOutput").ap()
    r_sp = nc.dram_tensor("r_sp", [128, CH, J], F32, kind="ExternalOutput").ap()
    g_sp = nc.dram_tensor("g_sp", [128, CH, TOK], F32, kind="ExternalOutput").ap()
    summ = nc.dram_tensor("summ", [3, 128, CH], F32, kind="ExternalOutput").ap()
    wg_b = nc.dram_tensor("wg_b", [FH, D], BF16, kind="Internal").ap()
    wu_b = nc.dram_tensor("wu_b", [FH, D], BF16, kind="Internal").ap()
    wd_b = nc.dram_tensor("wd_b", [D, FH], BF16, kind="Internal").ap()

    with ExitStack() as ctx:
        k = K(nc, ctx)
        for s in ("qs", "qg", "sv", "ss", "st", "sg"):
            k.sem(ctx, s)
        sb = lambda nm, shape, dt: ctx.enter_context(nc.sbuf_tensor(nm, shape, dt))
        psb = lambda nm, w: ctx.enter_context(nc.psum_tensor(nm, [128, w], F32))

        tabs = {}
        for nm, ap_ in vecs.items():
            tabs[nm] = sb(nm + "_t", [128, CH], F32)
            k.dma("sync", tabs[nm][:, :], ap_, "qt")
        negw = sb("negw", [128, CH], F32)
        hm_t = sb("hm_t", [128, 1], F32)
        kb_t = sb("kb_t", [128, 1], F32)
        eps_t = sb("eps_t", [128, 1], F32)
        id_t = sb("id_t", [128, 128], F32)
        ramp_t = sb("ramp_t", [128, 128], F32)
        k.dma("sync", hm_t[:, :], hmask, "qt")
        k.dma("sync", kb_t[:, :], kbias, "qt")
        k.dma("sync", eps_t[:, :], epsin, "qt")
        k.dma("sync", id_t[:, :], ident, "qt")
        k.dma("sync", ramp_t[:, :], ramp, "qt")
        s1 = sb("s1", [128, 1], F32)
        s2 = sb("s2", [128, 1], F32)
        st1 = sb("st1", [128, 1], F32)
        st2 = sb("st2", [128, 1], F32)
        st3 = sb("st3", [128, 1], F32)
        h1_carry = sb("h1_carry", [128, CH], F32)
        ps_a = psb("ps_a", D)
        ps_b = psb("ps_b", D)
        ps_m = [psb(f"ps_m{i}", 512) for i in range(2)]

        k.wait_s("scalar", "qt", k.now("qt"))
        k.op("scalar", lambda h: h.activation(
            negw[:, :], tabs["tdec"][:, :], AF.Exp), inc="ss")

        k.dma("gpsimd", wg_b, wg_i, "qgc")
        k.dma("gpsimd", wu_b, wu_i, "qgc")
        k.dma("gpsimd", wd_b, wd_i, "qgc")
        ffn_cast_done = k.now("qgc")

        A = Phase(nc)
        wk_s = A.sb("wk_s", [128, CH, D], BF16)
        wv_s = A.sb("wv_s", [128, CH, D], BF16)
        wr_s = A.sb("wr_s", [128, CH, D], BF16)
        for w_hbm, w_sb in ((wk, wk_s), (wv, wv_s), (wr, wr_s)):
            k.dma("gpsimd", w_sb[:, :, :],
                  w_hbm.rearrange("(kk p) e -> p kk e", p=128), "qga")
        a1_w_loaded = k.now("qga")
        k.wait("tensor", "qga", a1_w_loaded)

        x_tm = A.sb("x_tm", [128, 4, D], F32)
        t_tm = A.sb("t_tm", [128, D], F32)
        scr_tm = A.sb("scr_tm", [128, D], F32)
        xn_cm = A.sb("xn_cm", [128, CH, 512], F32)
        h1_cm = A.sb("h1_cm", [128, CH, 512], F32)
        dh = A.sb("dh", [128, CH, 512], F32)
        u_b = [A.sb(f"u_b{i}", [128, CH, 512], BF16) for i in range(3)]
        out_cm = {"k": A.sb("kcm", [128, CH, 512], F32),
                  "v": A.sb("vcm", [128, CH, 512], F32),
                  "r": A.sb("rcm", [128, CH, 512], F32)}
        spill_t = {"k": k_sp, "v": v_sp, "r": r_sp}

        tiles = [(512 * t0, 512) for t0 in range(4)] + [(TOK, 2)]
        bank_ev = {0: 0, 1: 0}
        mm_ev = {0: 0, 1: 0, 2: 0}
        sp_ev = {"xn": 0, "k": 0, "v": 0, "r": 0}
        ln_last = 0
        tp_last = 0
        evac_last = 0
        for ti, (tl0, n) in enumerate(tiles):
            nb = (n + 127) // 128
            if ti:
                k.wait("sync", "ss", ln_last)
                k.wait("sync", "st", tp_last)
            for b in range(nb):
                npart = min(128, n - 128 * b)
                k.dma("sync", x_tm[:npart, b, :],
                      x_a1[tl0 + 128 * b: tl0 + 128 * b + npart, :], "qx")
            k.wait_s("scalar", "qx", k.now("qx"))
            for b in range(nb):
                npart = min(128, n - 128 * b)
                blk = x_tm[:npart, b, :]
                ln_tokmajor(k, blk, blk, npart, scr_tm, s1, s2, st1, st2,
                            st3, eps_t)
                k.wait("scalar", "st", tp_last)
                xn_done = ln_tokmajor(k, blk, t_tm[:npart, :], npart,
                                      scr_tm, s1, s2, st1, st2, st3, eps_t)
                k.wait("tensor", "ss", xn_done)
                k.wait("tensor", "sv", evac_last)
                for c in range(CH):
                    k.op("tensor", lambda h, c=c, npart=npart, blk=blk: h.transpose(
                        ps_a[:, 128 * c:128 * c + npart],
                        blk[:, 128 * c:128 * (c + 1)], id_t[:npart, :npart]),
                        inc="st")
                for c in range(CH):
                    k.op("tensor", lambda h, c=c, npart=npart: h.transpose(
                        ps_b[:, 128 * c:128 * c + npart],
                        t_tm[:npart, 128 * c:128 * (c + 1)], id_t[:npart, :npart]),
                        inc="st")
                tp_last = k.now("st")
                k.wait("vector", "st", tp_last)
                if b == 0:
                    k.wait_s("vector", "qsxn", sp_ev["xn"])
                ps_a_v = ps_a.ap().rearrange("p (c q) -> p c q", c=CH)[:, :, :npart]
                k.op("vector", lambda h, b=b, npart=npart, s=ps_a_v: h.tensor_copy(
                    xn_cm.ap()[:, :, 128 * b:128 * b + npart], s), inc="sv")
                for c in range(CH):
                    k.op("vector", lambda h, c=c, b=b, npart=npart: h.tensor_scalar(
                        h1_cm[:, c, 128 * b:128 * b + npart],
                        ps_b[:, 128 * c:128 * c + npart],
                        tabs["ln1_g"][:, c:c + 1], tabs["ln1_b"][:, c:c + 1],
                        ALU.mult, ALU.add), inc="sv")
                evac_last = k.now("sv")
            ln_last = k.now("ss")
            if ti == 0:
                with k.af():
                    k.op("vector", lambda h: h.tensor_scalar(
                        h1_cm[:, :, 1:2], h1_cm[:, :, 1:2], hm_t[:, 0:1], None,
                        ALU.mult), inc="sv")
            k.wait("sync", "sv", k.now("sv"))
            if tl0 == 0:
                k.dma("sync", xn_sp[:, :, 0:n - 1], xn_cm.ap()[:, :, 1:n],
                      "qsxn")
            else:
                k.dma("sync", xn_sp[:, :, tl0 - 1:tl0 - 1 + n],
                      xn_cm.ap()[:, :, 0:n], "qsxn")
            sp_ev["xn"] = k.now("qsxn")

            u0 = max(1, tl0)
            cnt = tl0 + n - u0
            off = u0 - tl0
            j0 = u0 - 1
            if off == 0:
                with k.af():
                    k.op("vector", lambda h: h.tensor_sub(
                        dh[:, :, 0:1], h1_cm[:, :, 0:1],
                        h1_carry.ap().unsqueeze(2)), inc="sv")
                if cnt > 1:
                    k.op("vector", lambda h, cnt=cnt: h.tensor_sub(
                        dh[:, :, 1:cnt], h1_cm[:, :, 1:cnt],
                        h1_cm[:, :, 0:cnt - 1]), inc="sv")
            else:
                k.op("vector", lambda h, off=off, cnt=cnt: h.tensor_sub(
                    dh[:, :, 0:cnt], h1_cm[:, :, off:off + cnt],
                    h1_cm[:, :, off - 1:off - 1 + cnt]), inc="sv")

            for mi, (mtn, w_sb, spname) in enumerate((
                    ("mix_k", wk_s, "k"), ("mix_v", wv_s, "v"),
                    ("mix_r", wr_s, "r"))):
                mt = tabs[mtn]
                ub = u_b[mi]
                ocm = out_cm[spname]
                k.wait("vector", "st", mm_ev[mi])
                for c in range(CH):
                    if off == 0:
                        with k.af():
                            k.op("vector", lambda h, c=c, mt=mt, ub=ub:
                                 h.scalar_tensor_tensor(
                                     ub[:, c, 0:1], dh[:, c, 0:1], mt[:, c:c + 1],
                                     h1_carry.ap()[:, c:c + 1].unsqueeze(2),
                                     ALU.mult, ALU.add), inc="sv")
                        if cnt > 1:
                            k.op("vector", lambda h, c=c, mt=mt, ub=ub, cnt=cnt:
                                 h.scalar_tensor_tensor(
                                     ub[:, c, 1:cnt], dh[:, c, 1:cnt],
                                     mt[:, c:c + 1], h1_cm[:, c, 0:cnt - 1],
                                     ALU.mult, ALU.add), inc="sv")
                    else:
                        k.op("vector", lambda h, c=c, mt=mt, ub=ub, off=off,
                             cnt=cnt: h.scalar_tensor_tensor(
                                 ub[:, c, 0:cnt], dh[:, c, 0:cnt],
                                 mt[:, c:c + 1],
                                 h1_cm[:, c, off - 1:off - 1 + cnt],
                                 ALU.mult, ALU.add), inc="sv")
                mix_done = k.now("sv")
                k.wait("tensor", "sv", mix_done)
                for o in range(CH):
                    bank = ps_m[o % 2]
                    k.wait("tensor", "sv", bank_ev[o % 2])
                    for kt in range(CH):
                        k.op("tensor", lambda h, o=o, kt=kt, w_sb=w_sb, ub=ub,
                             cnt=cnt, bank=bank: h.matmul(
                                 bank[:, 0:cnt],
                                 w_sb[:, kt, 128 * o:128 * (o + 1)],
                                 ub[:, kt, 0:cnt],
                                 start=(kt == 0), stop=(kt == 7)), inc="st")
                    k.wait("vector", "st", k.now("st"))
                    if o == 0:
                        k.wait_s("vector", "qs" + spname, sp_ev[spname])
                    k.op("vector", lambda h, o=o, ocm=ocm, cnt=cnt, bank=bank:
                         h.tensor_copy(ocm[:, o, 0:cnt], bank[:, 0:cnt]),
                         inc="sv")
                    bank_ev[o % 2] = k.now("sv")
                mm_ev[mi] = k.now("st")
                last_evac = k.now("sv")
                if spname == "k" and ti == 0:
                    with k.af():
                        k.op("vector", lambda h: h.tensor_scalar(
                            out_cm["k"][:, :, 0:1], out_cm["k"][:, :, 0:1],
                            hm_t[:, 0:1], kb_t[:, 0:1], ALU.mult, ALU.add),
                            inc="sv")
                    last_evac = k.now("sv")
                k.wait("sync", "sv", last_evac)
                k.dma("sync", spill_t[spname][:, :, j0:j0 + cnt],
                      ocm.ap()[:, :, 0:cnt], "qs" + spname)
                sp_ev[spname] = k.now("qs" + spname)
            with k.af():
                k.op("vector", lambda h, n=n: h.tensor_copy(
                    h1_carry[:, :], h1_cm.ap()[:, :, n - 1]), inc="sv")
        A.close()

        S = Phase(nc)
        sA = S.sb("sA", [128, CH], F32)
        sB = S.sb("sB", [128, CH], F32)
        sP = S.sb("sP", [128, CH], F32)
        tA = S.sb("tA", [128, CH], F32)
        tB = S.sb("tB", [128, CH], F32)
        tP = S.sb("tP", [128, CH], F32)
        f1 = S.sb("f1", [128, CH], F32)
        f2 = S.sb("f2", [128, CH], F32)
        m_buf = S.sb("m_buf", [128, CH, 128], F32)
        e_buf = S.sb("e_buf", [128, CH, 128], F32)
        kg_b = S.sb("kg_b", [128, CH, 128], F32)
        vg_b = S.sb("vg_b", [128, CH, 128], F32)
        negM = S.sb("negM", [128, CH], F32)
        M_b = S.sb("M_b", [128, CH], F32)
        k.wait("sync", "sv", k.now("sv"))
        k.wait("sync", "ss", k.now("ss"))
        k.wait("sync", "st", k.now("st"))
        k.autofence = True
        k.op("vector", lambda h: h.memset(sA[:, :], 0.0), inc="sv")
        k.op("vector", lambda h: h.memset(sB[:, :], 0.0), inc="sv")
        k.op("vector", lambda h: h.memset(sP[:, :], NEG), inc="sv")
        for g in range(16):
            if g:
                k.wait("sync", "sv", k.now("sv"))
                k.wait("sync", "ss", k.now("ss"))
            k.dma("sync", kg_b[:, :, :], k_sp[:, :, 128 * g:128 * (g + 1)],
                  "qsm")
            k.dma("sync", vg_b[:, :, :], v_sp[:, :, 128 * g:128 * (g + 1)],
                  "qsm")
            k.wait_s("vector", "qsm", k.now("qsm"))
            for c in range(CH):
                k.op("vector", lambda h, c=c: h.scalar_tensor_tensor(
                    m_buf[:, c, :], ramp_t[:, :], negw[:, c:c + 1],
                    kg_b[:, c, :], ALU.mult, ALU.add), inc="sv")
                k.op("vector", lambda h, c=c: h.reduce_max(
                    M_b[:, c:c + 1], m_buf[:, c, :], axis=AX), inc="sv")
            k.op("vector", lambda h: h.tensor_scalar(
                negM[:, :], M_b[:, :], -1.0, None, ALU.mult), inc="sv")
            k.wait("scalar", "sv", k.now("sv"))
            for c in range(CH):
                k.op("scalar", lambda h, c=c: h.activation(
                    e_buf[:, c, :], m_buf[:, c, :], AF.Exp,
                    bias=negM[:, c:c + 1], accum_out=tB[:, c:c + 1]), inc="ss")
            k.wait("vector", "ss", k.now("ss"))
            for c in range(CH):
                k.op("vector", lambda h, c=c: h.tensor_mul(
                    e_buf[:, c, :], e_buf[:, c, :], vg_b[:, c, :]), inc="sv")
                k.op("vector", lambda h, c=c: h.reduce_sum(
                    tA[:, c:c + 1], e_buf[:, c, :], axis=AX), inc="sv")
            k.op("vector", lambda h: h.scalar_tensor_tensor(
                tP[:, :], negw[:, :], 127.0, M_b[:, :], ALU.mult,
                ALU.subtract), inc="sv")
            k.op("vector", lambda h: h.tensor_scalar(
                tP[:, :], tP[:, :], -1.0, None, ALU.mult), inc="sv")
            k.op("vector", lambda h: h.scalar_tensor_tensor(
                f1[:, :], negw[:, :], -128.0, sP[:, :], ALU.mult, ALU.add),
                inc="sv")
            k.op("vector", lambda h: h.tensor_max(
                f2[:, :], f1[:, :], tP[:, :]), inc="sv")
            k.op("vector", lambda h: h.tensor_sub(f1[:, :], f1[:, :], f2[:, :]),
                 inc="sv")
            k.op("vector", lambda h: h.tensor_sub(tP[:, :], tP[:, :], f2[:, :]),
                 inc="sv")
            k.wait("scalar", "sv", k.now("sv"))
            k.op("scalar", lambda h: h.activation(f1[:, :], f1[:, :], AF.Exp),
                 inc="ss")
            k.op("scalar", lambda h: h.activation(tP[:, :], tP[:, :], AF.Exp),
                 inc="ss")
            k.wait("vector", "ss", k.now("ss"))
            for acc, tnew in ((sA, tA), (sB, tB)):
                k.op("vector", lambda h, acc=acc: h.tensor_mul(
                    acc[:, :], acc[:, :], f1[:, :]), inc="sv")
                k.op("vector", lambda h, tnew=tnew: h.tensor_mul(
                    tnew[:, :], tnew[:, :], tP[:, :]), inc="sv")
                k.op("vector", lambda h, acc=acc, tnew=tnew: h.tensor_add(
                    acc[:, :], acc[:, :], tnew[:, :]), inc="sv")
            k.op("vector", lambda h: h.tensor_copy(sP[:, :], f2[:, :]), inc="sv")
        k.autofence = False
        k.wait("sync", "sv", k.now("sv"))
        k.dma("sync", summ[0, :, :], sA[:, :], "qsum")
        k.dma("sync", summ[1, :, :], sB[:, :], "qsum")
        k.dma("sync", summ[2, :, :], sP[:, :], "qsum")
        S.close()

        F = Phase(nc)
        wgT = F.sb("wgT", [128, CH, FH], BF16)
        wuT = F.sb("wuT", [128, CH, FH], BF16)
        wdT = F.sb("wdT", [128, FHT, D], BF16)
        k.wait_s("sync", "qgc", ffn_cast_done)
        k.wait("sync", "sv", k.now("sv"))
        k.wait("sync", "ss", k.now("ss"))
        k.ensure("qw")
        for c in range(CH):
            k.op("sync", lambda h, c=c: h.dma_start(
                out=wgT[:, c, :], in_=wg_b[:, 128 * c:128 * (c + 1)],
                transpose=True), inc="qw", n=16)
            k.op("sync", lambda h, c=c: h.dma_start(
                out=wuT[:, c, :], in_=wu_b[:, 128 * c:128 * (c + 1)],
                transpose=True), inc="qw", n=16)
        for hc in range(FHT):
            k.op("sync", lambda h, hc=hc: h.dma_start(
                out=wdT[:, hc, :], in_=wd_b[:, 128 * hc:128 * (hc + 1)],
                transpose=True), inc="qw", n=16)
        ffn_w_loaded = k.now("qw")
        k.wait("tensor", "qw", ffn_w_loaded)
        k.wait("gpsimd", "qw", ffn_w_loaded)
        k.wait_s("gpsimd", "qsxn", k.now("qsxn"))

        xn_bf = F.sb("xn_bf", [128, CH, 512], BF16)
        hh_bf = F.sb("hh_bf", [128, FHT, 512], BF16)
        sg_scr = F.sb("sg_scr", [128, 512], F32)
        g_ev = F.sb("g_ev", [128, 2, 512], F32)
        ps_g = psb("ps_g", 512)
        ps_u = psb("ps_u", 512)
        gu_banks = [(ps_g.ap()[:, :], ps_u.ap()[:, :]),
                    (ps_a.ap()[:, 0:512], ps_a.ap()[:, 512:1024])]
        gev_ev = {0: 0, 1: 0}
        gmul_ev = {0: 0, 1: 0}
        gu_ev = {0: 0, 1: 0}
        a2mm_prev = 0
        k.wait("tensor", "sv", k.now("sv"))
        for b2 in range(4):
            m0 = 512 * b2
            k.wait("gpsimd", "st", a2mm_prev)
            k.dma("gpsimd", xn_bf[:, :, :], xn_sp[:, :, m0 + 1:m0 + 513],
                  "qgx")
            k.wait_s("tensor", "qgx", k.now("qgx"))
            for ho in range(FHT):
                bg, bu = gu_banks[ho % 2]
                k.wait("tensor", "sv", gu_ev[ho % 2])
                for kt in range(CH):
                    k.op("tensor", lambda h, ho=ho, kt=kt, bg=bg: h.matmul(
                        bg, wgT[:, kt, 128 * ho:128 * (ho + 1)],
                        xn_bf[:, kt, :], start=(kt == 0), stop=(kt == 7)),
                        inc="st")
                g_mm = k.now("st")
                for kt in range(CH):
                    k.op("tensor", lambda h, ho=ho, kt=kt, bu=bu: h.matmul(
                        bu, wuT[:, kt, 128 * ho:128 * (ho + 1)],
                        xn_bf[:, kt, :], start=(kt == 0), stop=(kt == 7)),
                        inc="st")
                u_mm = k.now("st")
                k.wait("vector", "st", g_mm)
                k.op("vector", lambda h, bg=bg: h.tensor_scalar(
                    sg_scr[:, :], bg, -1.0 / 127.0, 25.0,
                    ALU.mult, ALU.min), inc="sv")
                k.wait("scalar", "sv", k.now("sv"))
                k.op("scalar", lambda h: h.activation(
                    sg_scr[:, :], sg_scr[:, :], AF.Exp), inc="ss")
                k.op("scalar", lambda h: h.activation(
                    sg_scr[:, :], sg_scr[:, :], AF.Ln, bias=1.0), inc="ss")
                k.op("scalar", lambda h: h.activation(
                    sg_scr[:, :], sg_scr[:, :], AF.Exp, scale=-1.0), inc="ss")
                k.wait("vector", "ss", k.now("ss"))
                k.wait("vector", "st", u_mm)
                k.op("vector", lambda h, bg=bg: h.tensor_mul(
                    sg_scr[:, :], sg_scr[:, :], bg), inc="sv")
                k.op("vector", lambda h, ho=ho, bu=bu: h.tensor_mul(
                    hh_bf[:, ho, :], sg_scr[:, :], bu), inc="sv")
                gu_ev[ho % 2] = k.now("sv")
            hh_done = k.now("sv")
            k.wait("tensor", "sv", hh_done)
            for do in range(CH):
                bank = ps_m[do % 2]
                k.wait("tensor", "sv", gmul_ev[do % 2])
                for kt in range(FHT):
                    k.op("tensor", lambda h, do=do, kt=kt, bank=bank: h.matmul(
                        bank[:, :], wdT[:, kt, 128 * do:128 * (do + 1)],
                        hh_bf[:, kt, :], start=(kt == 0), stop=(kt == FHT - 1)),
                        inc="st")
                d_mm = k.now("st")
                k.wait("vector", "st", d_mm)
                k.op("vector", lambda h, bank=bank: h.tensor_scalar(
                    sg_scr[:, :], bank[:, :], -C127_3, 25.0,
                    ALU.mult, ALU.min), inc="sv")
                k.wait("scalar", "sv", k.now("sv"))
                k.op("scalar", lambda h: h.activation(
                    sg_scr[:, :], sg_scr[:, :], AF.Exp), inc="ss")
                k.op("scalar", lambda h: h.activation(
                    sg_scr[:, :], sg_scr[:, :], AF.Ln, bias=1.0), inc="ss")
                k.op("scalar", lambda h: h.activation(
                    sg_scr[:, :], sg_scr[:, :], AF.Exp, scale=-1.0), inc="ss")
                k.wait("vector", "ss", k.now("ss"))
                slot = do % 2
                k.wait_s("vector", "qgs%d" % slot, gev_ev[slot])
                k.op("vector", lambda h, slot=slot, bank=bank: h.tensor_mul(
                    g_ev[:, slot, :], sg_scr[:, :], bank[:, :]), inc="sv")
                gmul_ev[slot] = k.now("sv")
                k.wait("sync", "sv", k.now("sv"))
                k.dma("sync", g_sp[:, do, m0:m0 + 512], g_ev.ap()[:, slot, :],
                      "qgs%d" % slot)
                gev_ev[slot] = k.now("qgs%d" % slot)
            a2mm_prev = k.now("st")
        F.close()

        for snm in list(k.sems):
            if snm.startswith("q"):
                k.wait("sync", snm, k.now(snm))
        block = ctx.enter_context(nc.Block())
        k.emit(block)
    return nc


def build_launch2():
    nc = bass.Bass("TRN2", target_bir_lowering=False)
    x_b = nc.dram_tensor("x_b", [TOK, D], F32, kind="ExternalInput").ap()
    xn_sp = nc.dram_tensor("xn_sp", [128, CH, J], F32, kind="ExternalInput").ap()
    k_sp = nc.dram_tensor("k_sp", [128, CH, J], F32, kind="ExternalInput").ap()
    v_sp = nc.dram_tensor("v_sp", [128, CH, J], F32, kind="ExternalInput").ap()
    r_sp = nc.dram_tensor("r_sp", [128, CH, J], F32, kind="ExternalInput").ap()
    g_sp = nc.dram_tensor("g_sp", [128, CH, TOK], F32, kind="ExternalInput").ap()
    summ_all = nc.dram_tensor("summ_all", [128, NCORES, 3, CH], F32,
                              kind="ExternalInput").ap()
    wo = nc.dram_tensor("wo", [D, D], F32, kind="ExternalInput").ap()
    cwr = nc.dram_tensor("cwr", [D, D], F32, kind="ExternalInput").ap()
    cwk = nc.dram_tensor("cwk", [D, FF], F32, kind="ExternalInput").ap()
    cwv = nc.dram_tensor("cwv", [FF, D], F32, kind="ExternalInput").ap()
    vecs = {}
    for nm in ("ln2_g", "ln2_b", "cmix_k", "cmix_r", "tdec", "tfirst"):
        vecs[nm] = nc.dram_tensor(nm, [128, CH], F32, kind="ExternalInput").ap()
    hmask = nc.dram_tensor("hmask", [128, 1], F32, kind="ExternalInput").ap()
    epsin = nc.dram_tensor("epsin", [128, 1], F32, kind="ExternalInput").ap()
    cmask = nc.dram_tensor("cmask", [128, NCORES], F32, kind="ExternalInput").ap()
    ident = nc.dram_tensor("ident", [128, 128], F32, kind="ExternalInput").ap()
    out = nc.dram_tensor("out", [TOK, D], F32, kind="ExternalOutput").ap()
    xmid_sp = nc.dram_tensor("xmid_sp", [128, CH, J], F32, kind="Internal").ap()
    uck_sp = nc.dram_tensor("uck_sp", [128, CH, TOK], BF16, kind="Internal").ap()
    rr_sp = nc.dram_tensor("rr_sp", [128, CH, TOK], F32, kind="Internal").ap()
    a2rr_sp = nc.dram_tensor("a2rr_sp", [128, CH, TOK], F32, kind="Internal").ap()

    with ExitStack() as ctx:
        k = K(nc, ctx)
        for s in ("qs", "qg", "sv", "ss", "st", "sg"):
            k.sem(ctx, s)
        sb = lambda nm, shape, dt: ctx.enter_context(nc.sbuf_tensor(nm, shape, dt))
        psb = lambda nm, w: ctx.enter_context(nc.psum_tensor(nm, [128, w], F32))

        tabs = {}
        for nm, ap_ in vecs.items():
            tabs[nm] = sb(nm + "_t", [128, CH], F32)
            k.dma("sync", tabs[nm][:, :], ap_, "qt")
        negw = sb("negw", [128, CH], F32)
        w_t = sb("w_t", [128, CH], F32)
        U_t = sb("U_t", [128, CH], F32)
        hm_t = sb("hm_t", [128, 1], F32)
        eps_t = sb("eps_t", [128, 1], F32)
        cm_t = sb("cm_t", [128, NCORES], F32)
        id_t = sb("id_t", [128, 128], F32)
        k.dma("sync", hm_t[:, :], hmask, "qt")
        k.dma("sync", eps_t[:, :], epsin, "qt")
        k.dma("sync", cm_t[:, :], cmask, "qt")
        k.dma("sync", id_t[:, :], ident, "qt")
        s1 = sb("s1", [128, 1], F32)
        s2 = sb("s2", [128, 1], F32)
        st1 = sb("st1", [128, 1], F32)
        st2 = sb("st2", [128, 1], F32)
        st3 = sb("st3", [128, 1], F32)
        cPP = sb("cPP", [128, CH], F32)
        cAA = sb("cAA", [128, CH], F32)
        cBB = sb("cBB", [128, CH], F32)
        cH2 = sb("cH2", [128, CH], F32)
        ps_a = psb("ps_a", D)
        ps_b = psb("ps_b", D)
        ps_m = [psb(f"ps_m{i}", 512) for i in range(2)]
        ps_k = psb("ps_k", 512)

        k.wait_s("scalar", "qt", k.now("qt"))
        k.op("scalar", lambda h: h.activation(
            negw[:, :], tabs["tdec"][:, :], AF.Exp), inc="ss")
        k.op("scalar", lambda h: h.activation(
            U_t[:, :], tabs["tfirst"][:, :], AF.Exp), inc="ss")
        k.wait("vector", "ss", k.now("ss"))
        k.op("vector", lambda h: h.tensor_scalar(
            w_t[:, :], negw[:, :], -1.0, None, ALU.mult), inc="sv")

        G = Phase(nc)
        summ_b = G.sb("summ_b", [128, NCORES, 3, CH], F32)
        f1 = G.sb("f1", [128, CH], F32)
        f2 = G.sb("f2", [128, CH], F32)
        f3 = G.sb("f3", [128, CH], F32)
        f4 = G.sb("f4", [128, CH], F32)
        f5 = G.sb("f5", [128, CH], F32)
        k.dma("sync", summ_b[:, :, :, :], summ_all, "qt2")
        k.autofence = True
        k.op("vector", lambda h: h.memset(cAA[:, :], 0.0), inc="sv")
        k.op("vector", lambda h: h.memset(cBB[:, :], 0.0), inc="sv")
        k.op("vector", lambda h: h.memset(cPP[:, :], NEG), inc="sv")
        k.wait_s("vector", "qt2", k.now("qt2"))
        k.wait("vector", "ss", 2)
        for jc in range(NCORES - 1):
            k.op("vector", lambda h: h.scalar_tensor_tensor(
                f1[:, :], negw[:, :], -2048.0, cPP[:, :], ALU.mult, ALU.add),
                inc="sv")
            k.op("vector", lambda h, jc=jc: h.tensor_max(
                f2[:, :], f1[:, :], summ_b[:, jc, 2, :]), inc="sv")
            k.op("vector", lambda h: h.tensor_sub(
                f1[:, :], f1[:, :], f2[:, :]), inc="sv")
            k.op("vector", lambda h, jc=jc: h.tensor_sub(
                f3[:, :], summ_b[:, jc, 2, :], f2[:, :]), inc="sv")
            k.wait("scalar", "sv", k.now("sv"))
            k.op("scalar", lambda h: h.activation(f1[:, :], f1[:, :], AF.Exp),
                 inc="ss")
            k.op("scalar", lambda h: h.activation(f3[:, :], f3[:, :], AF.Exp),
                 inc="ss")
            k.wait("vector", "ss", k.now("ss"))
            for acc, idx in ((cAA, 0), (cBB, 1)):
                k.op("vector", lambda h, acc=acc: h.tensor_mul(
                    f4[:, :], acc[:, :], f1[:, :]), inc="sv")
                k.op("vector", lambda h, jc=jc, idx=idx: h.tensor_mul(
                    f5[:, :], summ_b[:, jc, idx, :], f3[:, :]), inc="sv")
                k.op("vector", lambda h: h.tensor_add(
                    f4[:, :], f4[:, :], f5[:, :]), inc="sv")
                k.op("vector", lambda h, acc=acc: h.tensor_sub(
                    f4[:, :], f4[:, :], acc[:, :]), inc="sv")
                k.op("vector", lambda h, acc=acc, jc=jc: h.scalar_tensor_tensor(
                    acc[:, :], f4[:, :], cm_t[:, jc:jc + 1], acc[:, :],
                    ALU.mult, ALU.add), inc="sv")
            k.op("vector", lambda h: h.tensor_sub(
                f4[:, :], f2[:, :], cPP[:, :]), inc="sv")
            k.op("vector", lambda h, jc=jc: h.scalar_tensor_tensor(
                cPP[:, :], f4[:, :], cm_t[:, jc:jc + 1], cPP[:, :],
                ALU.mult, ALU.add), inc="sv")
        k.autofence = False
        fold_sv = k.now("sv")
        fold_ss = k.now("ss")
        G.close()

        B = Phase(nc)
        wo_s = B.sb("wo_s", [128, CH, D], BF16)
        k.wait("gpsimd", "sv", fold_sv)
        k.wait("gpsimd", "ss", fold_ss)
        k.dma("gpsimd", wo_s[:, :, :],
              wo.rearrange("(kk p) e -> p kk e", p=128), "qgw")
        b1_w_loaded = k.now("qgw")
        NB = 256
        bK = [B.sb(f"bK{i}", [128, CH, NB], F32) for i in range(2)]
        bV = [B.sb(f"bV{i}", [128, CH, NB], F32) for i in range(2)]
        bR = [B.sb(f"bR{i}", [128, CH, NB], F32) for i in range(2)]
        bXN = [B.sb(f"bXN{i}", [128, CH, NB], F32) for i in range(2)]
        bPP = B.sb("bPP", [128, CH, NB], F32)
        bQ = B.sb("bQ", [128, CH, NB], F32)
        bD = B.sb("bD", [128, CH, NB], F32)
        bAA = B.sb("bAA", [128, CH, NB], F32)
        bBB = B.sb("bBB", [128, CH, NB], F32)
        bRW = B.sb("bRW", [128, CH, NB], BF16)
        k.wait("sync", "sv", fold_sv)
        k.wait("sync", "ss", fold_ss)

        tiles = [(NB * t0, NB) for t0 in range(TOK // NB)] + [(TOK, 1)]
        ev_bK = {0: 0, 1: 0}
        ev_bV = {0: 0, 1: 0}
        ev_bR = {0: 0, 1: 0}
        ev_bXN_qs = {0: 0, 1: 0}
        ev_bXN_st = {0: 0, 1: 0}
        xm_ev = {0: 0, 1: 0}
        ev_womm = 0
        ev_uckspill = 0
        ev_cwrmm = 0
        ev_ln2ln = 0
        ev_h2tp = 0
        ev_h2evac = 0
        ev_tsclamp = 0
        rr_ev = {0: 0, 1: 0}
        first_mm = True
        for ti, (j0, n) in enumerate(tiles):
            p = ti % 2
            k.autofence = (n < 128)
            slow = {"allow_slow_non_contiguous": True} if n == 1 else {}
            qldp = "qsa" if p == 0 else "qsb"
            k.wait("sync", "sv", ev_bK[p])
            k.dma("sync", bK[p][:, :, 0:n], k_sp[:, :, j0:j0 + n], qldp, **slow)
            k.wait("sync", "sv", ev_bV[p])
            k.dma("sync", bV[p][:, :, 0:n], v_sp[:, :, j0:j0 + n], qldp, **slow)
            k.wait("sync", "sv", ev_bR[p])
            k.dma("sync", bR[p][:, :, 0:n], r_sp[:, :, j0:j0 + n], qldp, **slow)
            k.wait_s("sync", "qm%d" % p, ev_bXN_qs[p])
            k.wait("sync", "st", ev_bXN_st[p])
            k.dma("sync", bXN[p][:, :, 0:n], xn_sp[:, :, j0:j0 + n], qldp,
                  **slow)
            loads = k.now(qldp)
            k.wait_s("vector", qldp, loads)
            for c in range(CH):
                k.op("vector", lambda h, c=c, n=n, p=p: h.tensor_tensor_scan(
                    bPP[:, c, 0:n],
                    w_t[:, c:c + 1].broadcast_to([128, n]),
                    bK[p][:, c, 0:n], cPP[:, c:c + 1], ALU.add, ALU.max),
                    inc="sv")
            with k.af():
                k.op("vector", lambda h, p=p: h.tensor_sub(
                    bQ[:, :, 0:1], cPP.ap().unsqueeze(2), bPP[:, :, 0:1]),
                    inc="sv")
            if n > 1:
                k.op("vector", lambda h, n=n: h.tensor_sub(
                    bQ[:, :, 1:n], bPP[:, :, 0:n - 1], bPP[:, :, 1:n]),
                    inc="sv")
            k.op("vector", lambda h, n=n, p=p: h.tensor_sub(
                bK[p][:, :, 0:n], bK[p][:, :, 0:n], bPP[:, :, 0:n]), inc="sv")
            k.wait("scalar", "sv", k.now("sv"))
            for c in range(CH):
                k.op("scalar", lambda h, c=c, n=n: h.activation(
                    bD[:, c, 0:n], bQ[:, c, 0:n], AF.Exp,
                    bias=w_t[:, c:c + 1]), inc="ss")
            k.op("scalar", lambda h, n=n: h.activation(
                bQ[:, :, 0:n], bQ[:, :, 0:n], AF.Exp), inc="ss")
            k.op("scalar", lambda h, n=n, p=p: h.activation(
                bK[p][:, :, 0:n], bK[p][:, :, 0:n], AF.Exp), inc="ss")
            k.wait("vector", "ss", k.now("ss"))
            k.op("vector", lambda h, n=n, p=p: h.tensor_mul(
                bV[p][:, :, 0:n], bK[p][:, :, 0:n], bV[p][:, :, 0:n]),
                inc="sv")
            for c in range(CH):
                k.op("vector", lambda h, c=c, n=n, p=p: h.tensor_tensor_scan(
                    bAA[:, c, 0:n], bD[:, c, 0:n], bV[p][:, c, 0:n],
                    cAA[:, c:c + 1], ALU.mult, ALU.add), inc="sv")
                k.op("vector", lambda h, c=c, n=n, p=p: h.tensor_tensor_scan(
                    bBB[:, c, 0:n], bD[:, c, 0:n], bK[p][:, c, 0:n],
                    cBB[:, c:c + 1], ALU.mult, ALU.add), inc="sv")
            with k.af():
                k.op("vector", lambda h: h.tensor_mul(
                    bD[:, :, 0:1], bQ[:, :, 0:1], cAA.ap().unsqueeze(2)),
                    inc="sv")
            if n > 1:
                k.op("vector", lambda h, n=n: h.tensor_mul(
                    bD[:, :, 1:n], bQ[:, :, 1:n], bAA[:, :, 0:n - 1]), inc="sv")
            for c in range(CH):
                k.op("vector", lambda h, c=c, n=n, p=p: h.scalar_tensor_tensor(
                    bD[:, c, 0:n], bV[p][:, c, 0:n], U_t[:, c:c + 1],
                    bD[:, c, 0:n], ALU.mult, ALU.add), inc="sv")
            with k.af():
                k.op("vector", lambda h, p=p: h.tensor_mul(
                    bV[p][:, :, 0:1], bQ[:, :, 0:1], cBB.ap().unsqueeze(2)),
                    inc="sv")
            if n > 1:
                k.op("vector", lambda h, n=n, p=p: h.tensor_mul(
                    bV[p][:, :, 1:n], bQ[:, :, 1:n], bBB[:, :, 0:n - 1]),
                    inc="sv")
            for c in range(CH):
                k.op("vector", lambda h, c=c, n=n, p=p: h.scalar_tensor_tensor(
                    bV[p][:, c, 0:n], bK[p][:, c, 0:n], U_t[:, c:c + 1],
                    bV[p][:, c, 0:n], ALU.mult, ALU.add), inc="sv")
            with k.af():
                k.op("vector", lambda h, n=n: h.tensor_copy(
                    cPP[:, :], bPP.ap()[:, :, n - 1]), inc="sv")
                k.op("vector", lambda h, n=n: h.tensor_copy(
                    cAA[:, :], bAA.ap()[:, :, n - 1]), inc="sv")
                k.op("vector", lambda h, n=n: h.tensor_copy(
                    cBB[:, :], bBB.ap()[:, :, n - 1]), inc="sv")
            k.op("vector", lambda h, n=n, p=p: h.tensor_scalar(
                bR[p][:, :, 0:n], bR[p][:, :, 0:n], -1.0, 20.0, ALU.mult,
                ALU.min), inc="sv")
            k.wait("scalar", "sv", k.now("sv"))
            k.op("scalar", lambda h, n=n, p=p: h.activation(
                bR[p][:, :, 0:n], bR[p][:, :, 0:n], AF.Exp), inc="ss")
            k.wait("vector", "ss", k.now("ss"))
            k.op("vector", lambda h, n=n, p=p: h.tensor_mul(
                bK[p][:, :, 0:n], bV[p][:, :, 0:n], bR[p][:, :, 0:n]),
                inc="sv")
            ev_bR[p] = k.now("sv")
            k.op("vector", lambda h, n=n, p=p: h.tensor_add(
                bV[p][:, :, 0:n], bV[p][:, :, 0:n], bK[p][:, :, 0:n]),
                inc="sv")
            ev_bK[p] = k.now("sv")
            k.wait("scalar", "sv", k.now("sv"))
            k.op("scalar", lambda h, n=n, p=p: h.activation(
                bV[p][:, :, 0:n], bV[p][:, :, 0:n], AF.Ln), inc="ss")
            k.op("scalar", lambda h, n=n, p=p: h.activation(
                bV[p][:, :, 0:n], bV[p][:, :, 0:n], AF.Exp, scale=-1.0),
                inc="ss")
            k.wait("vector", "ss", k.now("ss"))
            k.wait("vector", "st", ev_womm)
            k.op("vector", lambda h, n=n, p=p: h.tensor_mul(
                bRW[:, :, 0:n], bD[:, :, 0:n], bV[p][:, :, 0:n]), inc="sv")
            ev_bV[p] = k.now("sv")
            rw_done = k.now("sv")
            k.wait("tensor", "sv", rw_done)
            if first_mm:
                k.wait_s("tensor", "qgw", b1_w_loaded)
                first_mm = False
            for o in range(CH):
                bank = ps_m[o % 2]
                k.wait("tensor", "sv", xm_ev[o % 2])
                for kt in range(CH):
                    k.op("tensor", lambda h, o=o, kt=kt, n=n, bank=bank:
                         h.matmul(bank[:, 0:n],
                                  wo_s[:, kt, 128 * o:128 * (o + 1)],
                                  bRW[:, kt, 0:n], start=(kt == 0),
                                  stop=(kt == 7)), inc="st")
                k.wait("vector", "st", k.now("st"))
                k.op("vector", lambda h, o=o, n=n, p=p, bank=bank: h.tensor_add(
                    bXN[p][:, o, 0:n], bXN[p][:, o, 0:n], bank[:, 0:n]),
                    inc="sv")
                xm_ev[o % 2] = k.now("sv")
            ev_womm = k.now("st")
            xmid_done = k.now("sv")
            k.wait("sync", "sv", xmid_done)
            k.dma("sync", xmid_sp[:, :, j0:j0 + n],
                  bXN[p].ap()[:, :, 0:n], "qm%d" % p,
                  **({"allow_slow_non_contiguous": True} if n == 1 else {}))
            ev_bXN_qs[p] = k.now("qm%d" % p)
            k.autofence = False
        B.close()

        Bb = Phase(nc)
        cwr_s = Bb.sb("cwr_s", [128, CH, D], BF16)
        k.wait("gpsimd", "sv", k.now("sv"))
        k.wait("gpsimd", "ss", k.now("ss"))
        k.wait("gpsimd", "st", k.now("st"))
        for snm in ("qsa", "qsb", "qm0", "qm1"):
            k.wait_s("gpsimd", snm, k.now(snm))
        k.dma("gpsimd", cwr_s[:, :, :],
              cwr.rearrange("(kk p) e -> p kk e", p=128), "qgw")
        cwr_loaded = k.now("qgw")
        NBB = 512
        bXM = [Bb.sb(f"bXM{i}", [128, CH, NBB], F32) for i in range(2)]
        bH2 = Bb.sb("bH2", [128, CH, NBB], F32)
        bUCK = Bb.sb("bUCK", [128, CH, NBB], BF16)
        bUCR = Bb.sb("bUCR", [128, CH, NBB], BF16)
        bRRP = Bb.sb("bRRP", [128, CH, NBB], F32)
        tm_buf = Bb.sb("tm_buf", [128, D], F32)
        tm_scr = Bb.sb("tm_scr", [128, D], F32)
        k.wait("tensor", "sv", k.now("sv"))
        k.wait_s("tensor", "qgw", cwr_loaded)
        btiles = [(0, 1)] + [(1 + 512 * t0, 512) for t0 in range(4)]
        ev_xm_ld = {0: 0, 1: 0}
        ev_xm_st = {0: 0, 1: 0}
        ev_ln2ln = 0
        ev_h2tp = 0
        ev_h2evac = 0
        ev_uckspill = 0
        ev_cwrmm2 = 0
        ev_rrpev = 0
        ev_rrsig = 0
        for ti, (jl, n) in enumerate(btiles):
            pb = ti % 2
            k.autofence = (n < 128)
            slown = {"allow_slow_non_contiguous": True} if n == 1 else {}
            qld = "qbb%d" % pb
            k.wait("sync", "st", ev_xm_st[pb])
            k.wait("sync", "sv", ev_xm_ld[pb])
            k.dma("sync", bXM[pb][:, :, 0:n], xmid_sp[:, :, jl:jl + n], qld,
                  **slown)
            k.wait_s("tensor", qld, k.now(qld))
            nblk = (n + 127) // 128
            for tb in range(nblk):
                w = min(128, n - 128 * tb)
                k.wait("tensor", "ss", ev_ln2ln)
                for c in range(CH):
                    k.op("tensor", lambda h, c=c, tb=tb, w=w, pb=pb:
                         h.transpose(
                             ps_a[:w, 128 * c:128 * (c + 1)],
                             bXM[pb][:, c, 128 * tb:128 * tb + w], id_t[:, :]),
                         inc="st")
                tp_done = k.now("st")
                ev_xm_st[pb] = tp_done
                k.wait("scalar", "st", max(tp_done, ev_h2tp))
                ev_ln2ln = ln_tokmajor(k, ps_a[:w, :], tm_buf[:w, :], w,
                                       tm_scr, s1, s2, st1, st2, st3, eps_t)
                k.wait("tensor", "ss", ev_ln2ln)
                k.wait("tensor", "sv", ev_h2evac)
                for c in range(CH):
                    k.op("tensor", lambda h, c=c, w=w: h.transpose(
                        ps_b[:, 128 * c:128 * c + w],
                        tm_buf[:w, 128 * c:128 * (c + 1)], id_t[:w, :w]),
                        inc="st")
                ev_h2tp = k.now("st")
                k.wait("vector", "st", ev_h2tp)
                for c in range(CH):
                    k.op("vector", lambda h, c=c, tb=tb, w=w: h.tensor_scalar(
                        bH2[:, c, 128 * tb:128 * tb + w],
                        ps_b[:, 128 * c:128 * c + w],
                        tabs["ln2_g"][:, c:c + 1], tabs["ln2_b"][:, c:c + 1],
                        ALU.mult, ALU.add), inc="sv")
                ev_h2evac = k.now("sv")
            if ti == 0:
                with k.af():
                    k.op("vector", lambda h: h.tensor_scalar(
                        bH2[:, :, 0:1], bH2[:, :, 0:1], hm_t[:, 0:1], None,
                        ALU.mult), inc="sv")
                k.op("vector", lambda h: h.tensor_copy(
                    cH2[:, :], bH2.ap()[:, :, 0]), inc="sv")
                k.autofence = False
                continue
            for mtn, ub, war in (("cmix_k", bUCK, ("qu", ev_uckspill)),
                                 ("cmix_r", bUCR, ("st", ev_cwrmm2))):
                mt = tabs[mtn]
                k.wait_s("vector", war[0], war[1])
                for c in range(CH):
                    with k.af():
                        k.op("vector", lambda h, c=c: h.tensor_sub(
                            tm_scr[:, 0:1], bH2[:, c, 0:1],
                            cH2[:, c:c + 1]), inc="sv")
                        k.op("vector", lambda h, c=c, mt=mt, ub=ub:
                             h.scalar_tensor_tensor(
                                 ub[:, c, 0:1], tm_scr[:, 0:1],
                                 mt[:, c:c + 1], cH2[:, c:c + 1],
                                 ALU.mult, ALU.add), inc="sv")
                    k.op("vector", lambda h, c=c, n=n: h.tensor_sub(
                        bRRP[:, c, 1:n], bH2[:, c, 1:n], bH2[:, c, 0:n - 1]),
                        inc="sv")
                    k.op("vector", lambda h, c=c, mt=mt, ub=ub, n=n:
                         h.scalar_tensor_tensor(
                             ub[:, c, 1:n], bRRP[:, c, 1:n], mt[:, c:c + 1],
                             bH2[:, c, 0:n - 1], ALU.mult, ALU.add), inc="sv")
            mix_done = k.now("sv")
            with k.af():
                k.op("vector", lambda h, n=n: h.tensor_copy(
                    cH2[:, :], bH2.ap()[:, :, n - 1]), inc="sv")
            ev_xm_ld[pb] = k.now("sv")
            k.wait("sync", "sv", mix_done)
            k.dma("sync", uck_sp[:, :, jl - 1:jl - 1 + n],
                  bUCK.ap()[:, :, 0:n], "qu")
            ev_uckspill = k.now("qu")
            k.wait("tensor", "sv", mix_done)
            for o in range(CH):
                bank = ps_m[o % 2]
                k.wait("tensor", "sv", ev_rrpev)
                for kt in range(CH):
                    k.op("tensor", lambda h, o=o, kt=kt, n=n, bank=bank:
                         h.matmul(bank[:, 0:n],
                                  cwr_s[:, kt, 128 * o:128 * (o + 1)],
                                  bUCR[:, kt, 0:n], start=(kt == 0),
                                  stop=(kt == 7)), inc="st")
                k.wait("vector", "st", k.now("st"))
                k.wait("vector", "ss", ev_rrsig)
                k.op("vector", lambda h, o=o, n=n: h.tensor_scalar(
                    bRRP[:, o, 0:n], ps_m[o % 2][:, 0:n], -1.0, 20.0,
                    ALU.mult, ALU.min), inc="sv")
                ev_rrpev = k.now("sv")
            ev_cwrmm2 = k.now("st")
            k.wait("scalar", "sv", ev_rrpev)
            k.wait_s("scalar", "qrr0", k.now("qrr0"))
            k.op("scalar", lambda h, n=n: h.activation(
                bRRP[:, :, 0:n], bRRP[:, :, 0:n], AF.Exp), inc="ss")
            k.op("scalar", lambda h, n=n: h.activation(
                bRRP[:, :, 0:n], bRRP[:, :, 0:n], AF.Ln, bias=1.0), inc="ss")
            k.op("scalar", lambda h, n=n: h.activation(
                bRRP[:, :, 0:n], bRRP[:, :, 0:n], AF.Exp, scale=-1.0),
                inc="ss")
            ev_rrsig = k.now("ss")
            k.wait("sync", "ss", ev_rrsig)
            k.dma("sync", rr_sp[:, :, jl - 1:jl - 1 + n],
                  bRRP.ap()[:, :, 0:n], "qrr0")
        Bb.close()

        B2 = Phase(nc)
        cwk_s = B2.sb("cwk_s", [128, CH, FF], BF16)
        cwv_s = B2.sb("cwv_s", [128, FFT, D], BF16)
        k.wait("gpsimd", "sv", k.now("sv"))
        k.wait("gpsimd", "ss", k.now("ss"))
        k.wait("gpsimd", "st", k.now("st"))
        for snm in ("qsa", "qsb", "qm0", "qm1", "qu", "qrr0", "qrr1"):
            k.wait_s("gpsimd", snm, k.now(snm))
        k.dma("gpsimd", cwk_s[:, :, :],
              cwk.rearrange("(kk p) f -> p kk f", p=128), "qgw")
        k.dma("gpsimd", cwv_s[:, :, :],
              cwv.rearrange("(kk p) e -> p kk e", p=128), "qgw")
        b2_w_loaded = k.now("qgw")
        k.wait_s("tensor", "qgw", b2_w_loaded)
        bUK2 = [B2.sb(f"bUK2{i}", [128, CH, 512], BF16) for i in range(2)]
        _bRR2 = B2.sb("bRR2", [128, CH, 512], F32)
        bRR2 = [_bRR2, _bRR2]
        kk_bf = B2.sb("kk_bf", [128, FFT, 512], BF16)
        scr2 = B2.sb("scr2", [128, 512], F32)
        a2ev = B2.sb("a2ev", [128, 2, 512], F32)
        kk_banks = [ps_k.ap()[:, :], ps_b.ap()[:, 0:512]]
        relu_ev = {0: 0, 1: 0}
        a2_ev = {0: 0, 1: 0}
        a2tt_ev = {0: 0, 1: 0}
        ev_kkmm = {0: 0, 1: 0}
        ev_cvvmm = 0
        k.wait("tensor", "sv", k.now("sv"))
        for b2 in range(4):
            m0 = 512 * b2
            p = b2 % 2
            qb = "qb%d" % p
            k.wait("sync", "st", ev_kkmm[p])
            k.dma("sync", bUK2[p][:, :, :], uck_sp[:, :, m0:m0 + 512], qb)
            k.wait("sync", "sv", max(a2tt_ev[0], a2tt_ev[1]))
            k.dma("sync", bRR2[p][:, :, :], rr_sp[:, :, m0:m0 + 512], "qbr")
            k.wait_s("tensor", qb, k.now(qb))
            k.wait_s("vector", "qbr", k.now("qbr"))
            k.wait("scalar", "st", ev_cvvmm)
            for fo in range(FFT):
                bank = kk_banks[fo % 2]
                k.wait("tensor", "sv", relu_ev[fo % 2])
                for kt in range(CH):
                    k.op("tensor", lambda h, fo=fo, kt=kt, p=p, bank=bank:
                         h.matmul(bank, cwk_s[:, kt, 128 * fo:128 * (fo + 1)],
                                  bUK2[p][:, kt, :], start=(kt == 0),
                                  stop=(kt == 7)), inc="st")
                k.wait("vector", "st", k.now("st"))
                k.wait("vector", "ss", k.now("ss"))
                k.op("vector", lambda h, bank=bank: h.tensor_scalar(
                    scr2[:, :], bank, 0.0, None, ALU.max), inc="sv")
                relu_ev[fo % 2] = k.now("sv")
                k.wait("scalar", "sv", k.now("sv"))
                k.op("scalar", lambda h, fo=fo: h.activation(
                    kk_bf[:, fo, :], scr2[:, :], AF.Square), inc="ss")
            ev_kkmm[p] = k.now("st")
            kk_done = k.now("ss")
            k.wait("tensor", "ss", kk_done)
            for o in range(CH):
                bank = ps_m[o % 2]
                k.wait("tensor", "sv", a2tt_ev[o % 2])
                for kt in range(FFT):
                    k.op("tensor", lambda h, o=o, kt=kt, bank=bank: h.matmul(
                        bank[:, :], cwv_s[:, kt, 128 * o:128 * (o + 1)],
                        kk_bf[:, kt, :], start=(kt == 0), stop=(kt == FFT - 1)),
                        inc="st")
                k.wait("vector", "st", k.now("st"))
                slot = o % 2
                k.wait_s("vector", "qa%d" % slot, a2_ev[slot])
                k.op("vector", lambda h, o=o, p=p, slot=slot, bank=bank:
                     h.tensor_mul(a2ev[:, slot, :], bRR2[p][:, o, :],
                                  bank[:, :]), inc="sv")
                a2tt_ev[slot] = k.now("sv")
                k.wait("sync", "sv", k.now("sv"))
                k.dma("sync", a2rr_sp[:, o, m0:m0 + 512],
                      a2ev.ap()[:, slot, :], "qa%d" % slot)
                a2_ev[slot] = k.now("qa%d" % slot)
            ev_cvvmm = k.now("st")
        B2.close()

        B3 = Phase(nc)
        bXM3 = [B3.sb(f"bXM3{i}", [128, CH, 512], F32) for i in range(2)]
        bA23 = [B3.sb(f"bA23{i}", [128, CH, 512], F32) for i in range(2)]
        bG3 = [B3.sb(f"bG3{i}", [128, CH, 512], F32) for i in range(2)]
        x_tm3 = [B3.sb(f"x_tm3{i}", [128, 4, D], F32) for i in range(2)]
        o_tm = B3.sb("o_tm", [128, 2, D], F32)
        k.wait("sync", "sv", k.now("sv"))
        k.wait("sync", "ss", k.now("ss"))
        k.wait("sync", "st", k.now("st"))
        ev_b3v = {0: 0, 1: 0}
        ev_b3tp = {0: 0, 1: 0}
        ev_otm = {0: 0, 1: 0}
        tp3_ev = 0
        ev_b3stt = {0: 0, 1: 0}
        ev_sttlast = 0
        for b3 in range(4):
            m0 = 512 * b3
            p = b3 % 2
            qc = "qc%d" % p
            k.wait("sync", "sv", ev_b3stt[p])
            k.wait("sync", "st", ev_b3tp[p])
            k.dma("sync", bXM3[p][:, :, :], xmid_sp[:, :, m0 + 1:m0 + 513],
                  qc)
            k.dma("sync", bA23[p][:, :, :], a2rr_sp[:, :, m0:m0 + 512], qc)
            k.dma("sync", bG3[p][:, :, :], g_sp[:, :, m0:m0 + 512], qc)
            for b in range(4):
                k.dma("sync", x_tm3[p][:, b, :],
                      x_b[m0 + 128 * b:m0 + 128 * (b + 1), :], qc)
            k.wait_s("vector", qc, k.now(qc))
            k.op("vector", lambda h, p=p: h.tensor_add(
                bXM3[p][:, :, :], bXM3[p][:, :, :], bA23[p][:, :, :]),
                inc="sv")
            k.op("vector", lambda h, p=p: h.tensor_mul(
                bXM3[p][:, :, :], bXM3[p][:, :, :], bG3[p][:, :, :]),
                inc="sv")
            sdone = k.now("sv")
            ev_b3v[p] = sdone
            k.wait("tensor", "sv", sdone)
            for tb in range(4):
                ot_slot = tb % 2
                k.wait("tensor", "sv", ev_sttlast)
                for c in range(CH):
                    k.op("tensor", lambda h, c=c, tb=tb, p=p: h.transpose(
                        ps_a[:, 128 * c:128 * (c + 1)],
                        bXM3[p][:, c, 128 * tb:128 * (tb + 1)], id_t[:, :]),
                        inc="st")
                tp3_ev = k.now("st")
                k.wait("vector", "st", tp3_ev)
                k.wait_s("vector", "qo%d" % ot_slot, ev_otm[ot_slot])
                k.op("vector", lambda h, tb=tb, p=p, ot=ot_slot:
                     h.scalar_tensor_tensor(
                         o_tm[:, ot, :], ps_a[:, :], C127_3,
                         x_tm3[p][:, tb, :], ALU.mult, ALU.add), inc="sv")
                odone = k.now("sv")
                ev_sttlast = odone
                k.wait("sync", "sv", odone)
                k.dma("sync", out[m0 + 128 * tb:m0 + 128 * (tb + 1), :],
                      o_tm.ap()[:, ot_slot, :], "qo%d" % ot_slot)
                ev_otm[ot_slot] = k.now("qo%d" % ot_slot)
                ev_b3stt[p] = odone
            ev_b3tp[p] = tp3_ev
        B3.close()

        for snm in list(k.sems):
            if snm.startswith("q"):
                k.wait("sync", snm, k.now(snm))
        block = ctx.enter_context(nc.Block())
        k.emit(block)
    return nc


def _vec_layout(v):
    return np.ascontiguousarray(np.asarray(v, np.float32).reshape(CH, 128).T)


def kernel(L0_attn_x, L0_attn_ln1_g, L0_attn_ln1_b, L0_attn_ln2_g,
           L0_attn_ln2_b, L0_attn_tm_mix_k, L0_attn_tm_mix_v, L0_attn_tm_mix_r,
           L0_attn_time_decay, L0_attn_time_first, L0_attn_wk, L0_attn_wv,
           L0_attn_wr, L0_attn_wo, L0_attn_cm_mix_k, L0_attn_cm_mix_r,
           L0_attn_cwk, L0_attn_cwv, L0_attn_cwr, L0_ffn_w_gate, L0_ffn_w_up,
           L0_ffn_w_down):
    x = np.ascontiguousarray(np.asarray(L0_attn_x, np.float32).reshape(T, D))
    f32 = lambda a: np.ascontiguousarray(np.asarray(a, np.float32))
    i32 = lambda a: np.ascontiguousarray(np.asarray(a, np.int32))
    ident = np.eye(128, dtype=np.float32)
    ramp = np.ascontiguousarray(np.tile(np.arange(128, dtype=np.float32),
                                        (128, 1)))
    epsv = np.full((128, 1), EPS, np.float32)

    common1 = {
        "wk": f32(L0_attn_wk), "wv": f32(L0_attn_wv), "wr": f32(L0_attn_wr),
        "wg_i": i32(L0_ffn_w_gate), "wu_i": i32(L0_ffn_w_up),
        "wd_i": i32(L0_ffn_w_down),
        "ln1_g": _vec_layout(L0_attn_ln1_g), "ln1_b": _vec_layout(L0_attn_ln1_b),
        "mix_k": _vec_layout(L0_attn_tm_mix_k),
        "mix_v": _vec_layout(L0_attn_tm_mix_v),
        "mix_r": _vec_layout(L0_attn_tm_mix_r),
        "tdec": _vec_layout(L0_attn_time_decay),
        "ident": ident, "ramp": ramp, "epsin": epsv,
    }
    maps1 = []
    for i in range(NCORES):
        s = TOK * i
        xa = np.zeros((TOK + 2, D), np.float32)
        if i == 0:
            xa[2:] = x[0:TOK]
        else:
            xa[:] = x[s - 2:s + TOK]
        hm = np.zeros((128, 1), np.float32) if i == 0 else \
            np.ones((128, 1), np.float32)
        kb = np.full((128, 1), -1.0e30, np.float32) if i == 0 else \
            np.zeros((128, 1), np.float32)
        maps1.append({**common1, "x_a1": xa, "hmask": hm, "kbias": kb})

    nc1 = build_launch1()
    res1 = run_bass_kernel_spmd(nc1, maps1, core_ids=list(range(NCORES)))

    summ_all = np.stack([np.asarray(res1.results[i]["summ"])
                         for i in range(NCORES)], axis=0)
    summ_all = np.ascontiguousarray(summ_all.transpose(2, 0, 1, 3))

    common2 = {
        "wo": f32(L0_attn_wo), "cwr": f32(L0_attn_cwr),
        "cwk": f32(L0_attn_cwk), "cwv": f32(L0_attn_cwv),
        "ln2_g": _vec_layout(L0_attn_ln2_g), "ln2_b": _vec_layout(L0_attn_ln2_b),
        "cmix_k": _vec_layout(L0_attn_cm_mix_k),
        "cmix_r": _vec_layout(L0_attn_cm_mix_r),
        "tdec": _vec_layout(L0_attn_time_decay),
        "tfirst": _vec_layout(L0_attn_time_first),
        "ident": ident, "summ_all": summ_all, "epsin": epsv,
    }
    maps2 = []
    for i in range(NCORES):
        s = TOK * i
        hm = np.zeros((128, 1), np.float32) if i == 0 else \
            np.ones((128, 1), np.float32)
        cm = np.zeros((128, NCORES), np.float32)
        cm[:, :i] = 1.0
        r1 = res1.results[i]
        maps2.append({**common2, "x_b": np.ascontiguousarray(x[s:s + TOK]),
                      "hmask": hm, "cmask": cm,
                      "xn_sp": np.asarray(r1["xn_sp"]),
                      "k_sp": np.asarray(r1["k_sp"]),
                      "v_sp": np.asarray(r1["v_sp"]),
                      "r_sp": np.asarray(r1["r_sp"]),
                      "g_sp": np.asarray(r1["g_sp"])})

    nc2 = build_launch2()
    res2 = run_bass_kernel_spmd(nc2, maps2, core_ids=list(range(NCORES)))
    out = np.concatenate([np.asarray(res2.results[i]["out"])
                          for i in range(NCORES)], axis=0)
    return out.reshape(1, T, D)
